# version 09 (final)
# speedup vs baseline: 1.1228x; 1.0316x over previous
from contextlib import ExitStack

import numpy as np
import ml_dtypes

import concourse.bass as bass
import concourse.mybir as mybir
from concourse.bass_utils import run_bass_kernel_spmd

F32 = mybir.dt.float32
BF16 = mybir.dt.bfloat16
I32 = mybir.dt.int32
AF = mybir.ActivationFunctionType
ALU = mybir.AluOpType
AX = mybir.AxisListType
bfdt = ml_dtypes.bfloat16

NCORES = 8
B, T, C = 2, 4096, 2048
P = 1024
NCH = C // 128
ST = (slice(0, 512), slice(512, 1024))

NQ, NH, DH = 4, 8, 64
DKV = NH * DH
H1, DH1, W1 = 16, 128, 256

ENGINES = ("tensor", "vector", "scalar", "gpsimd", "sync")


class Sem:
    def __init__(self, nc, ctx, name):
        self.h = ctx.enter_context(nc.semaphore(name))
        self.n = 0


class Bld:

    def __init__(self, nc):
        self.nc = nc
        self.q = {e: [] for e in ENGINES}
        self._lastw = {e: {} for e in ENGINES}

    def op(self, eng, fn, waits=(), inc=None, dma=False, inc2=None):
        ws = []
        for w in waits:
            s, v = (w, w.n) if isinstance(w, Sem) else (w[0], w[1])
            if v <= 0:
                continue
            prev = self._lastw[eng].get(id(s), -1)
            if v > prev:
                self._lastw[eng][id(s)] = v
                ws.append((s.h, v))
        incs = []
        incs_nd = []
        for s, nd in ((inc, False), (inc2, True)):
            if s is not None:
                k = 16 if dma else 1
                (incs_nd if nd else incs).append((s.h, k))
                s.n += k
        self.q[eng].append((fn, ws, incs, incs_nd))

    def wait(self, eng, *sems):
        self.op(eng, None, waits=sems)

    def finalize(self, blk):
        def runner(entries, name):
            def run(e):
                for fn, ws, incs, incs_nd in entries:
                    for h, v in ws:
                        e.wait_ge(h, v)
                    if fn is None:
                        continue
                    ins = fn(e)
                    if name == "tensor":
                        for j, (h, k) in enumerate(incs_nd):
                            (ins if j == 0 else e.nop()).then_inc(h, k)
                        if incs:
                            dr = e.drain(fusable=True)
                            for j, (h, k) in enumerate(incs):
                                (dr if j == 0 else e.nop()).then_inc(h, k)
                    else:
                        for j, (h, k) in enumerate(incs + incs_nd):
                            (ins if j == 0 else e.nop()).then_inc(h, k)
            return run
        for name in ENGINES:
            if self.q[name]:
                getattr(blk, name)(runner(self.q[name], name))


def fm(dram2):
    return dram2.rearrange("(c p) t -> p c t", p=128)


def build_A():
    nc = bass.Bass("TRN2", target_bir_lowering=False)
    xT = nc.dram_tensor("xT", [C, P], F32, kind="ExternalInput").ap()
    wqT = nc.dram_tensor("wqT", [NCH, 128, C], F32, kind="ExternalInput").ap()
    wkT = nc.dram_tensor("wkT", [128, NCH * DKV], F32, kind="ExternalInput").ap()
    wvT = nc.dram_tensor("wvT", [128, NCH * DKV], F32, kind="ExternalInput").ap()
    memi = nc.dram_tensor("memi", [128, 4 * DKV], F32, kind="ExternalInput").ap()
    memn = nc.dram_tensor("memn", [128, 4], F32, kind="ExternalInput").ap()
    c_ones128 = nc.dram_tensor("c_ones128", [128, 1], F32, kind="ExternalInput").ap()
    c_ones1 = nc.dram_tensor("c_ones1", [1, 128], F32, kind="ExternalInput").ap()
    c_eps = nc.dram_tensor("c_eps", [1, 1], F32, kind="ExternalInput").ap()

    o_xn = nc.dram_tensor("xn", [C, P], BF16, kind="ExternalOutput").ap()
    o_xnf = nc.dram_tensor("xnf", [C, P], F32, kind="ExternalOutput").ap()
    o_q = nc.dram_tensor("q", [C, P], BF16, kind="ExternalOutput").ap()
    o_am = nc.dram_tensor("am", [C, P], BF16, kind="ExternalOutput").ap()
    o_k = nc.dram_tensor("k", [DKV, P], BF16, kind="ExternalOutput").ap()
    o_vt = nc.dram_tensor("vt", [P, DKV], BF16, kind="ExternalOutput").ap()

    ctx = ExitStack()
    E = ctx.enter_context
    sbt, pst = nc.sbuf_tensor, nc.psum_tensor
    xb = E(sbt("xb", [128, NCH, P], BF16))
    sqr = E(sbt("sqr", [128, 2, P], BF16))
    xnb = E(sbt("xnb", [128, NCH, P], BF16))
    sqb = E(sbt("sqb", [128, NCH, P], BF16))
    xf32 = E(sbt("xf32", [128, 2, P], F32))
    wvb = E(sbt("wvb", [128, NCH, DKV], BF16))
    wtile = E(sbt("wtile", [128, 2, NCH, 128], BF16))
    memb = E(sbt("memb", [128, 4, DKV], BF16))
    memnb = E(sbt("memnb", [128, 4, 1], BF16))
    onesb = E(sbt("onesb", [128, 1], BF16))
    ones1f = E(sbt("ones1f", [1, 128], F32))
    epsb = E(sbt("epsb", [1, 1], F32))
    mrow = E(sbt("mrow", [1, P], F32))
    vrow = E(sbt("vrow", [1, P], F32))
    rrow = E(sbt("rrow", [1, P], F32))
    drow = E(sbt("drow", [1, 512], F32))
    mbc = E(sbt("mbc", [128, P], F32))
    rbc = E(sbt("rbc", [128, P], F32))
    dbc = E(sbt("dbc", [128, 512], F32))
    scrx = E(sbt("scrx", [128, P], F32))
    xstage = E(sbt("xstage", [128, 2, P], F32))
    scra = E(sbt("scra", [128, 2, 512], F32))
    scrb = E(sbt("scrb", [128, 2, 512], F32))
    scrc = E(sbt("scrc", [128, 2, 512], F32))
    obuf = E(sbt("obuf", [128, 2, P], BF16))
    vtb = E(sbt("vtb", [128, 2, DKV], BF16))
    amb = E(sbt("amb", [128, 2, 512], BF16))

    p_mm = E(pst("p_mm", [128, 4, 512], F32))
    p_bc = E(pst("p_bc", [128, 2, 512], F32))

    s_gld = Sem(nc, ctx, "s_gld")
    s_yld = Sem(nc, ctx, "s_yld")
    s_t = Sem(nc, ctx, "s_t")
    s_v = Sem(nc, ctx, "s_v")
    s_s = Sem(nc, ctx, "s_s")
    s_st = Sem(nc, ctx, "s_st")
    s_wfree = Sem(nc, ctx, "s_wfree")

    b = Bld(nc)

    b.op("gpsimd", lambda g: g.dma_start(out=xb[:, :, :], in_=fm(xT)), inc=s_gld, dma=True)
    b.op("gpsimd", lambda g: g.dma_start(out=wvb[:, :, :], in_=wvT.rearrange("p (c f) -> p c f", c=NCH)), inc=s_gld, dma=True)
    b.op("gpsimd", lambda g: g.dma_start(out=memb[:, :, :], in_=memi.rearrange("p (c f) -> p c f", c=4)), inc=s_gld, dma=True)
    b.op("gpsimd", lambda g: g.dma_start(out=memnb[:, :, 0], in_=memn), inc=s_gld, dma=True)
    b.op("gpsimd", lambda g: g.dma_start(out=onesb[:], in_=c_ones128), inc=s_gld, dma=True)
    LOADS0 = (s_gld, s_gld.n)
    b.op("sync", lambda sy: sy.dma_start(out=ones1f[:], in_=c_ones1), inc=s_yld, dma=True)
    b.op("sync", lambda sy: sy.dma_start(out=epsb[:], in_=c_eps), inc=s_yld, dma=True)
    CONSTS0 = (s_yld, s_yld.n)

    def stats_mm(t):
        for st in range(2):
            for c in range(NCH):
                ins = t.matmul(p_mm[0:1, st, :], onesb[:], xb[:, c, ST[st]],
                               start=(c == 0), stop=(c == NCH - 1))
        return ins
    b.op("tensor", stats_mm, waits=[LOADS0], inc=s_t)

    for c in range(NCH):
        b.op("scalar", (lambda cc: lambda s: s.activation(
            sqr[:, cc % 2, :], xb[:, cc, :], AF.Square))(c),
            waits=[LOADS0] + ([(s_t, c)] if c >= 2 else []), inc=s_s)
        def sqmm(cc):
            def go(t):
                for st in range(2):
                    ins = t.matmul(p_mm[0:1, 2 + st, :], onesb[:], sqr[:, cc % 2, ST[st]],
                                   start=(cc == 0), stop=(cc == NCH - 1))
                return ins
            return go
        b.op("tensor", sqmm(c), waits=[s_s], inc=s_t)
    STATS_MM = (s_t, s_t.n)

    b.op("vector", lambda v: v.tensor_scalar_mul(
        mrow[:], p_mm[0:1, 0:2, :].rearrange("p a b -> p (a b)"), 1.0 / C),
        waits=[STATS_MM], inc=s_v)
    b.op("vector", lambda v: v.tensor_tensor(vrow[:], mrow[:], mrow[:], ALU.mult),
         waits=[s_v], inc=s_v)
    b.op("vector", lambda v: v.scalar_tensor_tensor(
        vrow[:], p_mm[0:1, 2:4, :].rearrange("p a b -> p (a b)"), 1.0 / C,
        vrow[:], op0=ALU.mult, op1=ALU.subtract),
        waits=[s_v], inc=s_v)
    VAR_DONE = (s_v, s_v.n)
    b.op("scalar", lambda s: s.activation(vrow[:], vrow[:], AF.Sqrt, bias=epsb[:]),
         waits=[VAR_DONE, CONSTS0], inc=s_s)
    b.op("vector", lambda v: v.reciprocal(rrow[:], vrow[:]), waits=[s_s], inc=s_v)
    RSTD_DONE = (s_v, s_v.n)

    def bc_mm(t):
        for st in range(2):
            t.matmul(p_bc[:, st, :], ones1f[:], mrow[:, ST[st]], start=True, stop=True)
        for st in range(2):
            ins = t.matmul(p_mm[:, st, :], ones1f[:], rrow[:, ST[st]], start=True, stop=True)
        return ins
    b.op("tensor", bc_mm, waits=[RSTD_DONE], inc=s_t)
    BC_MM = (s_t, s_t.n)

    def bc_ev(s):
        s.copy(mbc[:], p_bc[:, 0:2, :].rearrange("p a b -> p (a b)"))
        return s.copy(rbc[:], p_mm[:, 0:2, :].rearrange("p a b -> p (a b)"))
    b.op("scalar", bc_ev, waits=[BC_MM], inc=s_s)
    BC_DONE = (s_s, s_s.n)

    for c in range(NCH):
        b.op("sync", (lambda cc: lambda sy: sy.dma_start(
            out=xf32[:, cc % 2, :], in_=xT[cc * 128:(cc + 1) * 128, :]))(c),
            waits=([(s_v, s_v.n - 3)] if c >= 2 else []), inc=s_yld, dma=True)
        b.op("vector", (lambda cc: lambda v: v.tensor_tensor(
            scrx[:], xf32[:, cc % 2, :], mbc[:], ALU.subtract))(c),
            waits=[BC_DONE, s_yld], inc=s_v)
        b.op("vector", (lambda cc: lambda v: v.tensor_tensor(
            xstage[:, cc % 2, :], scrx[:], rbc[:], ALU.mult))(c),
            waits=[s_v] + ([(s_st, s_st.n - 16)] if c >= 2 else []), inc=s_v)
        b.op("vector", (lambda cc: lambda v: v.tensor_copy(
            xnb[:, cc, :], xstage[:, cc % 2, :]))(c),
            waits=[s_v], inc=s_v)
        b.op("sync", (lambda cc: lambda sy: sy.dma_start(
            out=o_xn[cc * 128:(cc + 1) * 128, :], in_=xnb[:, cc, :]))(c),
            waits=[s_v], inc=s_st, dma=True)
        b.op("sync", (lambda cc: lambda sy: sy.dma_start(
            out=o_xnf[cc * 128:(cc + 1) * 128, :], in_=xstage[:, cc % 2, :]))(c),
            waits=[s_v], inc=s_st, dma=True)
    XN_DONE = (s_v, s_v.n)

    for f in range(NCH + 4):
        is_q = f < NCH
        src = (wqT[:, f * 128:(f + 1) * 128] if is_q
               else wkT[:, (f - NCH) * 128:(f - NCH + 1) * 128])
        b.op("gpsimd", (lambda s_, ff: lambda g: g.dma_start(
            out=wtile[:, ff % 2, :, :], in_=fm(s_)))(src, f),
            waits=([(s_wfree, f - 1)] if f >= 2 else []), inc=s_gld, dma=True)

        def mmproj(ff):
            def go(t):
                for st in range(2):
                    for c in range(NCH):
                        ins = t.matmul(p_mm[:, 2 * (ff % 2) + st, :],
                                       wtile[:, ff % 2, c, :], xnb[:, c, ST[st]],
                                       start=(c == 0), stop=(c == NCH - 1))
                return ins
            return go
        b.op("tensor", mmproj(f), waits=[s_gld, XN_DONE, s_s, s_v],
             inc=s_t, inc2=s_wfree)
        MM_F = (s_t, s_t.n)

        if is_q:
            for st in range(2):
                b.op("vector", (lambda ff, ss: lambda v: v.tensor_scalar_min(
                    scra[:, ss, :], p_mm[:, 2 * (ff % 2) + ss, :], 0.0))(f, st),
                    waits=[MM_F, s_s], inc=s_v)
                b.op("vector", (lambda ff, ss: lambda v: v.tensor_scalar_max(
                    scrc[:, ss, :], p_mm[:, 2 * (ff % 2) + ss, :], 0.0))(f, st),
                    waits=[MM_F], inc=s_v)
                b.op("scalar", (lambda ff, ss: lambda s: s.activation(
                    scrb[:, ss, :], scra[:, ss, :], AF.Exp))(f, st),
                    waits=[MM_F, s_v], inc=s_s)
                b.op("vector", (lambda ff, ss: lambda v: v.scalar_tensor_tensor(
                    sqb[:, ff, ST[ss]], scrb[:, ss, :], -1.0, scrc[:, ss, :],
                    op0=ALU.add, op1=ALU.add))(f, st),
                    waits=[s_s, s_v], inc=s_v)
        def ev2(ff, dst):
            def go(s):
                s.copy(obuf[:, ff % 2, 0:512], p_mm[:, 2 * (ff % 2) + 0, :])
                return s.copy(obuf[:, ff % 2, 512:1024], p_mm[:, 2 * (ff % 2) + 1, :])
            return go
        b.op("scalar", ev2(f, None), waits=[MM_F, s_st], inc=s_s)
        dst = (o_q[f * 128:(f + 1) * 128, :] if is_q
               else o_k[(f - NCH) * 128:(f - NCH + 1) * 128, :])
        b.op("sync", (lambda d_, ff: lambda sy: sy.dma_start(
            out=d_, in_=obuf[:, ff % 2, :]))(dst, f),
            waits=[s_s], inc=s_st, dma=True)
    SQ_ALL = (s_v, s_v.n)

    for tc in range(8):
        def vmm(tt):
            def go(t):
                for c in range(NCH):
                    ins = t.matmul(p_mm[:, tt % 2, :],
                                   xnb[:, c, tt * 128:(tt + 1) * 128], wvb[:, c, :],
                                   start=(c == 0), stop=(c == NCH - 1))
                return ins
            return go
        b.op("tensor", vmm(tc), waits=[XN_DONE, s_s, s_v], inc=s_t)
        b.op("scalar", (lambda tt: lambda s: s.copy(
            vtb[:, tt % 2, :], p_mm[:, tt % 2, :]))(tc),
            waits=[s_t, s_st], inc=s_s)
        b.op("sync", (lambda tt: lambda sy: sy.dma_start(
            out=o_vt[tt * 128:(tt + 1) * 128, :], in_=vtb[:, tt % 2, :]))(tc),
            waits=[s_s], inc=s_st, dma=True)

    for qg in range(NQ):
        for st in range(2):
            def dmm(qq, ss):
                def go(t):
                    for ci in range(4):
                        ins = t.matmul(p_bc[0:1, 1, :], memnb[:, ci, :],
                                       sqb[:, qq * 4 + ci, ST[ss]],
                                       start=(ci == 0), stop=(ci == 3))
                    return ins
                return go
            b.op("tensor", dmm(qg, st), waits=[SQ_ALL, s_v, s_s], inc=s_t)
            b.op("vector", lambda v: v.reciprocal(drow[:], p_bc[0:1, 1, :]),
                 waits=[s_t], inc=s_v)
            b.op("tensor", lambda t: t.matmul(p_bc[:, 0, :], ones1f[:], drow[:],
                                              start=True, stop=True),
                 waits=[s_v, s_s], inc=s_t)
            b.op("scalar", lambda s: s.copy(dbc[:], p_bc[:, 0, :]),
                 waits=[s_t], inc=s_s)
            DBC = (s_s, s_s.n)
            for fo in range(4):
                def amm(qq, ss, ff):
                    def go(t):
                        for ci in range(4):
                            ins = t.matmul(p_mm[:, 2 + (ff % 2), :],
                                           memb[:, ci, ff * 128:(ff + 1) * 128],
                                           sqb[:, qq * 4 + ci, ST[ss]],
                                           start=(ci == 0), stop=(ci == 3))
                        return ins
                    return go
                b.op("tensor", amm(qg, st, fo), waits=[s_v], inc=s_t)
                b.op("vector", (lambda ff: lambda v: v.tensor_tensor(
                    amb[:, ff % 2, :], p_mm[:, 2 + (ff % 2), :], dbc[:], ALU.mult))(fo),
                    waits=[s_t, DBC, s_st], inc=s_v)
                b.op("sync", (lambda qq, ss, ff: lambda sy: sy.dma_start(
                    out=o_am[qq * 512 + ff * 128:qq * 512 + (ff + 1) * 128, ST[ss]],
                    in_=amb[:, ff % 2, :]))(qg, st, fo),
                    waits=[s_v], inc=s_st, dma=True)

    b.wait("sync", s_st)
    b.finalize(ctx.enter_context(nc.Block()))
    ctx.close()
    return nc



TKC = T // 128


def build_B():
    nc = bass.Bass("TRN2", target_bir_lowering=False)
    qT = nc.dram_tensor("qT", [C, P], BF16, kind="ExternalInput").ap()
    kf = nc.dram_tensor("kf", [DKV, T], BF16, kind="ExternalInput").ap()
    va = nc.dram_tensor("va", [T, NH * 65], BF16, kind="ExternalInput").ap()
    c_ones = nc.dram_tensor("c_ones", [128, 128], F32, kind="ExternalInput").ap()
    o_s = nc.dram_tensor("adot", [C, P], F32, kind="ExternalOutput").ap()

    ctx = ExitStack()
    E = ctx.enter_context
    sbt, pst = nc.sbuf_tensor, nc.psum_tensor
    qb = E(sbt("qb", [128, NCH, P], BF16))
    kb = E(sbt("kb", [128, 4, T], BF16))
    vab = E(sbt("vab", [128, TKC, NH * 65], BF16))
    onesf = E(sbt("onesf", [128, 128], F32))
    ptile = E(sbt("ptile", [128, 2, 2, 512], BF16))
    rr = E(sbt("rr", [128, 2, 512], F32))
    albcr = E(sbt("albcr", [128, 2, 512], F32))
    albc = E(sbt("albc", [128, 2, 512], F32))
    stage = E(sbt("stage", [128, 2, 2, 512], F32))

    p_qk = E(pst("p_qk", [128, 2, 512], F32))
    p_pv = E(pst("p_pv", [128, 4, 512], F32))
    p_al = E(pst("p_al", [128, 2, 512], F32))

    s_gld = Sem(nc, ctx, "s_gld")
    s_t = Sem(nc, ctx, "s_t")
    s_v = Sem(nc, ctx, "s_v")
    s_s = Sem(nc, ctx, "s_s")
    s_st = Sem(nc, ctx, "s_st")

    b = Bld(nc)
    b.op("gpsimd", lambda g: g.dma_start(out=qb[:, :, :], in_=fm(qT)), inc=s_gld, dma=True)
    b.op("gpsimd", lambda g: g.dma_start(out=kb[:, :, :], in_=fm(kf)), inc=s_gld, dma=True)
    b.op("gpsimd", lambda g: g.dma_start(
        out=vab[:, :, :], in_=va.rearrange("(tc p) f -> p tc f", p=128)), inc=s_gld, dma=True)
    b.op("gpsimd", lambda g: g.dma_start(out=onesf[:], in_=c_ones), inc=s_gld, dma=True)
    LOADS = (s_gld, s_gld.n)

    combo = 0
    for st in range(2):
        for qg in range(NQ):
            for hp in range(NH // 2):
                j = qg * 4 + hp
                hA, hB = 2 * hp, 2 * hp + 1
                cp = combo % 2
                for tc in range(TKC):
                    def qk(jj, tt, ss):
                        def go(t):
                            t.matmul(p_qk[:, 0, :], kb[0:64, jj % 4, tt * 128:(tt + 1) * 128],
                                     qb[0:64, jj, ST[ss]], start=True, stop=True,
                                     tile_position=(0, 0))
                            return t.matmul(p_qk[:, 1, :], kb[64:128, jj % 4, tt * 128:(tt + 1) * 128],
                                            qb[64:128, jj, ST[ss]], start=True, stop=True,
                                            tile_position=(64, 0))
                        return go
                    b.op("tensor", qk(j, tc, st) if False else (
                        lambda hh, tt, jj, ss: lambda t: (
                            t.matmul(p_qk[:, 0, :], kb[0:64, hh, tt * 128:(tt + 1) * 128],
                                     qb[0:64, jj, ST[ss]], start=True, stop=True,
                                     tile_position=(0, 0)),
                            t.matmul(p_qk[:, 1, :], kb[64:128, hh, tt * 128:(tt + 1) * 128],
                                     qb[64:128, jj, ST[ss]], start=True, stop=True,
                                     tile_position=(64, 0)))[-1])(hp, tc, j, st),
                        waits=[LOADS, s_s, s_v], inc=s_t)
                    b.op("scalar", (lambda tt: lambda s: s.activation(
                        ptile[:, tt % 2, :, :], p_qk[:, :, :], AF.Exp, scale=0.125))(tc),
                        waits=[s_t], inc=s_s)
                    b.op("tensor", (lambda tt, ha, hb, cc: lambda t: (
                        t.matmul(p_pv[0:65, 2 * cc, :], vab[:, tt, ha * 65:(ha + 1) * 65],
                                 ptile[:, tt % 2, 0, :], start=(tt == 0), stop=(tt == TKC - 1)),
                        t.matmul(p_pv[0:65, 2 * cc + 1, :], vab[:, tt, hb * 65:(hb + 1) * 65],
                                 ptile[:, tt % 2, 1, :], start=(tt == 0), stop=(tt == TKC - 1)))[-1])(
                            tc, hA, hB, cp),
                        waits=[s_s] + ([s_v] if tc == 0 else []),
                        inc=(s_t if tc == TKC - 1 else None))
                PV_END = (s_t, s_t.n)
                b.op("scalar", (lambda cc: lambda s: (
                    s.copy(rr[64:65, 0, :], p_pv[64:65, 2 * cc, :]),
                    s.copy(rr[64:65, 1, :], p_pv[64:65, 2 * cc + 1, :]))[-1])(cp),
                    waits=[PV_END], inc=s_s)
                b.op("tensor", lambda t: (
                    t.matmul(p_al[0:64, 0, :], onesf[64:65, 0:64], rr[64:65, 0, :],
                             start=True, stop=True, tile_position=(64, 0)),
                    t.matmul(p_al[0:64, 1, :], onesf[64:65, 0:64], rr[64:65, 1, :],
                             start=True, stop=True, tile_position=(64, 0)))[-1],
                    waits=[s_s], inc=s_t)
                b.op("scalar", lambda s: s.copy(albcr[0:64, :, :], p_al[0:64, :, :]),
                     waits=[s_t], inc=s_s)
                b.op("vector", lambda v: v.reciprocal(albc[0:64, :, :], albcr[0:64, :, :]),
                     waits=[s_s], inc=s_v)
                b.op("vector", (lambda cc: lambda v: v.tensor_tensor(
                    stage[0:64, cc, 0, :], p_pv[0:64, 2 * cc, :], albc[0:64, 0, :],
                    ALU.mult))(cp), waits=[s_v, s_st], inc=s_v)
                b.op("vector", (lambda cc: lambda v: v.tensor_tensor(
                    stage[0:64, cc, 1, :], p_pv[0:64, 2 * cc + 1, :], albc[0:64, 1, :],
                    ALU.mult))(cp), waits=[s_v], inc=s_v)
                b.op("sync", (lambda jj, ss, cc: lambda sy: sy.dma_start(
                    out=o_s[jj * 128:jj * 128 + 64, ST[ss]], in_=stage[0:64, cc, 0, :]))(j, st, cp),
                    waits=[s_v], inc=s_st, dma=True)
                b.op("sync", (lambda jj, ss, cc: lambda sy: sy.dma_start(
                    out=o_s[jj * 128 + 64:(jj + 1) * 128, ST[ss]], in_=stage[0:64, cc, 1, :]))(j, st, cp),
                    waits=[s_v], inc=s_st, dma=True)
                combo += 1

    b.wait("sync", s_st)
    b.finalize(ctx.enter_context(nc.Block()))
    ctx.close()
    return nc


def prep_B(inp, outA):
    ones = np.ones((128, 128), np.float32)
    kfull, vaug = [], []
    for bb in range(B):
        kcat = np.concatenate([np.asarray(outA[4 * bb + i]["k"]) for i in range(4)], axis=1)
        vcat = np.concatenate([np.asarray(outA[4 * bb + i]["vt"]) for i in range(4)], axis=0)
        va = np.empty((T, NH, 65), dtype=bfdt)
        va[:, :, 0:64] = vcat.reshape(T, NH, 64)
        va[:, :, 64] = np.asarray(1.0, dtype=bfdt)
        kfull.append(np.ascontiguousarray(kcat))
        vaug.append(np.ascontiguousarray(va.reshape(T, NH * 65)))
    maps = []
    for core, bb, t0 in _cores():
        maps.append(dict(qT=np.ascontiguousarray(np.asarray(outA[core]["q"])),
                         kf=kfull[bb], va=vaug[bb], c_ones=ones))
    return maps


DEBUG_C = False
DEBUG_FUSED = False
FM, FS = 1408, 5632
NFM, NFS = FM // 128, FS // 128
NE = 8
NROW = NFS + NE * NFM


def build_C():
    nc = bass.Bass("TRN2", target_bir_lowering=False)
    xnT = nc.dram_tensor("xnT", [C, P], BF16, kind="ExternalInput").ap()
    xnF = nc.dram_tensor("xnF", [C, P], F32, kind="ExternalInput").ap()
    gwT = nc.dram_tensor("gwT", [C, NE], F32, kind="ExternalInput").ap()
    sgwT = nc.dram_tensor("sgwT", [128, NCH], F32, kind="ExternalInput").ap()
    fc1T = nc.dram_tensor("fc1T", [NE, NFM, 128, C], F32, kind="ExternalInput").ap()
    fc2T = nc.dram_tensor("fc2T", [NE, NFM, 128, C], F32, kind="ExternalInput").ap()
    pjT = nc.dram_tensor("pjT", [NE, NCH, 128, FM], F32, kind="ExternalInput").ap()
    sf1T = nc.dram_tensor("sf1T", [NFS, 128, C], F32, kind="ExternalInput").ap()
    sf2T = nc.dram_tensor("sf2T", [NFS, 128, C], F32, kind="ExternalInput").ap()
    spT = nc.dram_tensor("spT", [NCH, 128, FS], F32, kind="ExternalInput").ap()
    c_eye = nc.dram_tensor("c_eye", [128, 128], F32, kind="ExternalInput").ap()
    c_onesq = nc.dram_tensor("c_onesq", [128, 128], F32, kind="ExternalInput").ap()
    c_sel = nc.dram_tensor("c_sel", [9, 9 * 128], F32, kind="ExternalInput").ap()
    o_g = nc.dram_tensor("g", [C, P], F32, kind="ExternalOutput").ap()
    if DEBUG_C:
        o_dct = nc.dram_tensor("dbg_combT", [9, 512], F32, kind="ExternalOutput").ap()
        o_dcm = nc.dram_tensor("dbg_comb_tm", [128, 4 * 9], F32, kind="ExternalOutput").ap()
        o_dcbc = nc.dram_tensor("dbg_cbc", [128, 2 * 512], F32, kind="ExternalOutput").ap()
        o_dhs = nc.dram_tensor("dbg_hs", [128, 4 * 512], F32, kind="ExternalOutput").ap()

    ctx = ExitStack()
    E = ctx.enter_context
    sbt, pst = nc.sbuf_tensor, nc.psum_tensor
    xnb = E(sbt("xnb", [128, NCH, P], BF16))
    hs = E(sbt("hs", [128, NROW, 512], BF16))
    fcr = E(sbt("fcr", [128, 2, NCH, 128], BF16))
    pjr = E(sbt("pjr", [128, 2, 22, 128], BF16))
    gwb = E(sbt("gwb", [128, NCH, NE], F32))
    xnfb = E(sbt("xnfb", [128, NCH // 2, 128], F32))
    sgwb = E(sbt("sgwb", [128, NCH, 1], BF16))
    eyeb = E(sbt("eyeb", [128, 128], F32))
    selb = E(sbt("selb", [9, 9 * 128], F32))
    comb_tm = E(sbt("comb_tm", [128, 4, 9], F32))
    pe_sb = E(sbt("pe_sb", [128, 8], F32))
    probs = E(sbt("probs", [128, 8], F32))
    maskd = E(sbt("maskd", [128, 8], F32))
    rows1 = E(sbt("rows1", [128, 6], F32))
    combT = E(sbt("combT", [9, 512], F32))
    cbc = E(sbt("cbc", [128, 2, 512], F32))
    sg_t = E(sbt("sg_t", [128, 512], F32))
    scr2 = E(sbt("scr2", [128, 512], F32))
    gstage = E(sbt("gstage", [128, 2, 512], F32))

    p_h = E(pst("p_h", [128, 4, 512], F32))
    p_y = E(pst("p_y", [128, 2, 512], F32))
    p_r = E(pst("p_r", [128, 2, 512], F32))

    s_gld = Sem(nc, ctx, "s_gld")
    s_yld = Sem(nc, ctx, "s_yld")
    s_t = Sem(nc, ctx, "s_t")
    s_v = Sem(nc, ctx, "s_v")
    s_s = Sem(nc, ctx, "s_s")
    s_st = Sem(nc, ctx, "s_st")
    s_wf = Sem(nc, ctx, "s_wf")
    s_pf = Sem(nc, ctx, "s_pf")

    b = Bld(nc)
    b.op("gpsimd", lambda g: g.dma_start(out=xnb[:, :, :], in_=fm(xnT)), inc=s_gld, dma=True)
    b.op("sync", lambda sy: sy.dma_start(out=gwb[:, :, :], in_=fm(gwT)), inc=s_yld, dma=True)
    b.op("gpsimd", lambda g: g.dma_start(out=sgwb[:, :, 0], in_=sgwT), inc=s_gld, dma=True)
    LOADS = (s_gld, s_gld.n)
    b.op("sync", lambda sy: sy.dma_start(out=eyeb[:], in_=c_eye), inc=s_yld, dma=True)
    b.op("sync", lambda sy: sy.dma_start(out=selb[:], in_=c_sel), inc=s_yld, dma=True)
    CONSTS = (s_yld, s_yld.n)

    nfc = 0
    npj = 0
    for st in range(2):
        for tcc in range(4):
            tsl = slice(st * 512 + tcc * 128, st * 512 + (tcc + 1) * 128)
            for half in range(2):
                b.op("sync", (lambda ts2, hh: lambda sy: sy.dma_start(
                    out=xnfb[:, :, :], in_=fm(xnF)[:, hh * 8:(hh + 1) * 8, ts2]))(tsl, half),
                    waits=[(s_t, s_t.n)], inc=s_yld, dma=True)
                def rmm(ti, hh):
                    def go(t):
                        for c in range(8):
                            ins = t.matmul(p_r[:, ti % 2, 0:8], xnfb[:, c, :],
                                           gwb[:, hh * 8 + c, :],
                                           start=(hh == 0 and c == 0),
                                           stop=(hh == 1 and c == 7))
                        return ins
                    return go
                b.op("tensor", rmm(tcc, half),
                     waits=[LOADS, s_yld] + ([s_s, s_v] if half == 0 else []),
                     inc=s_t)
            RMM = (s_t, s_t.n)
            b.op("vector", (lambda tt: lambda v: v.reduce_max(
                rows1[:, 0:1], p_r[:, tt % 2, 0:8], axis=AX.X, negate=True))(tcc),
                waits=[RMM], inc=s_v)
            b.op("scalar", (lambda tt: lambda s: s.activation(
                pe_sb[:], p_r[:, tt % 2, 0:8], AF.Exp, bias=rows1[:, 0:1]))(tcc),
                waits=[RMM, s_v], inc=s_s)
            b.op("vector", lambda v: v.reduce_sum(rows1[:, 1:2], pe_sb[:], axis=AX.X),
                 waits=[s_s], inc=s_v)
            b.op("vector", lambda v: v.reciprocal(rows1[:, 2:3], rows1[:, 1:2]),
                 waits=[s_v], inc=s_v)
            b.op("vector", lambda v: v.tensor_scalar_mul(probs[:], pe_sb[:], rows1[:, 2:3]),
                 waits=[s_v], inc=s_v)
            b.op("vector", lambda v: v.reduce_max(rows1[:, 3:4], probs[:], axis=AX.X),
                 waits=[s_v], inc=s_v)
            b.op("vector", lambda v: v.tensor_scalar(maskd[:], probs[:], rows1[:, 3:4],
                                                     None, op0=ALU.is_ge),
                 waits=[s_v], inc=s_v)
            b.op("vector", lambda v: v.scalar_tensor_tensor(maskd[:], maskd[:], -2.0,
                                                            probs[:], op0=ALU.mult, op1=ALU.add),
                 waits=[s_v], inc=s_v)
            b.op("vector", lambda v: v.reduce_max(rows1[:, 4:5], maskd[:], axis=AX.X),
                 waits=[s_v], inc=s_v)
            b.op("vector", lambda v: v.tensor_scalar(maskd[:], probs[:], rows1[:, 4:5],
                                                     None, op0=ALU.is_ge),
                 waits=[s_v], inc=s_v)
            b.op("vector", (lambda tt: lambda v: v.tensor_tensor(
                comb_tm[:, tt, 0:8], probs[:], maskd[:], ALU.mult))(tcc),
                waits=[s_v], inc=s_v)
            b.op("tensor", (lambda tt, ts2: lambda t: [
                t.matmul(p_r[:, tt % 2, 8:9], xnb[:, c, ts2], sgwb[:, c, :],
                         start=(c == 0), stop=(c == NCH - 1)) for c in range(NCH)][-1])(tcc, tsl),
                waits=[s_s, s_v], inc=s_t)
            b.op("scalar", (lambda tt: lambda s: s.activation(
                comb_tm[:, tt, 8:9], p_r[:, tt % 2, 8:9], AF.Sigmoid))(tcc),
                waits=[s_t], inc=s_s)
        CHAINS = (s_s, s_s.n)
        for tt in range(4):
            b.op("tensor", (lambda ti: lambda t: t.transpose(
                p_r[0:9, ti % 2, 0:128], comb_tm[:, ti, :], eyeb[:]))(tt),
                waits=[CHAINS, (s_v, s_v.n), CONSTS, s_s], inc=s_t)
            b.op("scalar", (lambda ti: lambda s: s.copy(
                combT[:, ti * 128:(ti + 1) * 128], p_r[0:9, ti % 2, 0:128]))(tt),
                waits=[s_t], inc=s_s)
        COMBT = (s_s, s_s.n)
        if DEBUG_C and st == 0:
            b.op("sync", lambda sy: sy.dma_start(out=o_dct, in_=combT[:, :]),
                 waits=[COMBT], inc=s_st, dma=True)
            b.op("sync", lambda sy: sy.dma_start(
                out=o_dcm, in_=comb_tm[:, :, :].rearrange("p a b -> p (a b)")),
                waits=[COMBT], inc=s_st, dma=True)

        for grp in range(1 + NE):
            shared = (grp == 0)
            nf = NFS if shared else NFM
            if not shared:
                e = grp - 1
                b.op("tensor", (lambda ee: lambda t: t.matmul(
                    p_r[:, ee % 2, :], selb[:, ee * 128:(ee + 1) * 128], combT[:, :],
                    start=True, stop=True))(e),
                    waits=[COMBT, s_s], inc=s_t)
                b.op("scalar", (lambda ee: lambda s: s.copy(
                    cbc[:, ee % 2, :], p_r[:, ee % 2, :]))(e),
                    waits=[s_t, s_v], inc=s_s)
            else:
                b.op("tensor", lambda t: t.matmul(
                    p_r[:, 0, :], selb[:, 8 * 128:9 * 128], combT[:, :],
                    start=True, stop=True), waits=[COMBT, s_s], inc=s_t)
                b.op("scalar", lambda s: s.copy(cbc[:, 0, :], p_r[:, 0, :]),
                     waits=[s_t], inc=s_s)
            CBC = (s_s, s_s.n)
            for f in range(nf):
                for which in range(2):
                    if shared:
                        src = (sf1T if which == 0 else sf2T)[:, f * 128:(f + 1) * 128]
                    else:
                        src = (fc1T if which == 0 else fc2T)[e, :, f * 128:(f + 1) * 128]
                    b.op("gpsimd", (lambda s_, l_: lambda g: g.dma_start(
                        out=fcr[:, l_ % 2, :, :], in_=fm(s_)))(src, nfc),
                        waits=([(s_wf, nfc - 1)] if nfc >= 2 else []), inc=s_gld, dma=True)
                    def hmm(l_, wh, fp, ss):
                        def go(t):
                            for c in range(NCH):
                                ins = t.matmul(p_h[:, 2 * fp + wh, :],
                                               fcr[:, l_ % 2, c, :], xnb[:, c, ST[ss]],
                                               start=(c == 0), stop=(c == NCH - 1))
                            return ins
                        return go
                    b.op("tensor", hmm(nfc, which, f % 2, st),
                         waits=[s_gld, s_s, s_v], inc=s_t, inc2=s_wf)
                    nfc += 1
                HMM = (s_t, s_t.n)
                b.op("scalar", (lambda fp: lambda s: s.activation(
                    sg_t[:], p_h[:, 2 * fp, :], AF.Silu))(f % 2),
                    waits=[HMM], inc=s_s)
                b.op("vector", (lambda fp: lambda v: v.tensor_tensor(
                    scr2[:], p_h[:, 2 * fp + 1, :], sg_t[:], ALU.mult))(f % 2),
                    waits=[s_s], inc=s_v)
                row = f if shared else NFS + (grp - 1) * NFM + f
                b.op("vector", (lambda rr_, cc: lambda v: v.tensor_tensor(
                    hs[:, rr_, :], scr2[:], cbc[:, cc, :], ALU.mult))(
                        row, 0 if shared else (grp - 1) % 2),
                    waits=[s_v, CBC, s_t], inc=s_v)
        HS_ALL = (s_v, s_v.n)
        if DEBUG_C and st == 0:
            b.op("sync", lambda sy: sy.dma_start(
                out=o_dcbc, in_=cbc[:, :, :].rearrange("p a b -> p (a b)")),
                waits=[HS_ALL], inc=s_st, dma=True)
            b.op("vector", lambda v: v.tensor_copy(scr2[:], hs[:, 0, :]), waits=[HS_ALL], inc=s_v)
            b.op("sync", lambda sy: sy.dma_start(out=o_dhs[:, 0:512], in_=scr2[:]),
                 waits=[s_v], inc=s_st, dma=True)
            b.op("vector", lambda v: v.tensor_copy(sg_t[:], hs[:, NFS, :]), waits=[(s_st, s_st.n)], inc=s_v)
            b.op("sync", lambda sy: sy.dma_start(out=o_dhs[:, 512:1024], in_=sg_t[:]),
                 waits=[s_v], inc=s_st, dma=True)

        for c in range(NCH):
            subs = [("sh", 0, 22), ("sh", 22, 22)] + [("e", e, NFM) for e in range(NE)]
            row0 = {}
            start_row = 0
            for si, (kind, idx, nrows) in enumerate(subs):
                if kind == "sh":
                    src = spT[idx * 128:(idx + 22) * 128, c * 128:(c + 1) * 128]
                    hrow = idx
                else:
                    src = pjT[idx, :, c * 128:(c + 1) * 128]
                    hrow = NFS + idx * NFM
                b.op("gpsimd", (lambda s_, l_, nr: lambda g: g.dma_start(
                    out=pjr[:, l_ % 2, 0:nr, :],
                    in_=s_.rearrange("(f p) c -> p f c", p=128)))(src, npj, nrows),
                    waits=([(s_pf, npj - 1)] if npj >= 2 else []), inc=s_gld, dma=True)
                def ymm(l_, nr, hr, cc, first, last):
                    def go(t):
                        for f in range(nr):
                            ins = t.matmul(p_y[:, cc % 2, :], pjr[:, l_ % 2, f, :],
                                           hs[:, hr + f, :],
                                           start=(first and f == 0),
                                           stop=(last and f == nr - 1))
                        return ins
                    return go
                b.op("tensor", ymm(npj, nrows, hrow, c, si == 0, si == len(subs) - 1),
                     waits=[s_gld, HS_ALL] + ([s_s] if si == 0 else []),
                     inc=(s_t if si == len(subs) - 1 else None), inc2=s_pf)
                npj += 1
            YMM = (s_t, s_t.n)
            b.op("scalar", (lambda cc: lambda s: s.activation(
                gstage[:, cc % 2, :], p_y[:, cc % 2, :], AF.Silu))(c),
                waits=[YMM, s_st], inc=s_s)
            b.op("sync", (lambda cc, ss: lambda sy: sy.dma_start(
                out=o_g[cc * 128:(cc + 1) * 128, ST[ss]], in_=gstage[:, cc % 2, :]))(c, st),
                waits=[s_s], inc=s_st, dma=True)

    b.wait("sync", s_st)
    b.finalize(ctx.enter_context(nc.Block()))
    ctx.close()
    return nc


def tsl_of(tt, st):
    return slice(st * 512 + tt * 128, st * 512 + (tt + 1) * 128)


def prep_C(inp, outA):
    gwT = np.ascontiguousarray(inp["L0_ffn_gate_w"].T)
    sgwT = np.ascontiguousarray(inp["L0_ffn_shared_gate_w"].reshape(16, 128).T)
    fc1T = np.ascontiguousarray(inp["L0_ffn_expert_fc1"].transpose(0, 2, 1))
    fc2T = np.ascontiguousarray(inp["L0_ffn_expert_fc2"].transpose(0, 2, 1))
    pjT = np.ascontiguousarray(inp["L0_ffn_expert_proj"].transpose(0, 2, 1))
    sf1T = np.ascontiguousarray(inp["L0_ffn_shared_fc1"].T)
    sf2T = np.ascontiguousarray(inp["L0_ffn_shared_fc2"].T)
    spT = np.ascontiguousarray(inp["L0_ffn_shared_proj"].T)
    eye = np.eye(128, dtype=np.float32)
    sel = np.zeros((9, 9 * 128), np.float32)
    for e in range(9):
        sel[e, e * 128:(e + 1) * 128] = 1.0
    maps = []
    for core, bb, t0 in _cores():
        maps.append(dict(xnT=np.ascontiguousarray(np.asarray(outA[core]["xn"])),
                         xnF=np.ascontiguousarray(np.asarray(outA[core]["xnf"])),
                         gwT=gwT, sgwT=sgwT, fc1T=fc1T, fc2T=fc2T, pjT=pjT,
                         sf1T=sf1T, sf2T=sf2T, spT=spT, c_eye=eye, c_sel=sel))
    return maps


def build_D():
    nc = bass.Bass("TRN2", target_bir_lowering=False)
    xT = nc.dram_tensor("xT", [C, P], F32, kind="ExternalInput").ap()
    adT = nc.dram_tensor("adT", [C, P], F32, kind="ExternalInput").ap()
    amT = nc.dram_tensor("amT", [C, P], BF16, kind="ExternalInput").ap()
    gT = nc.dram_tensor("gT", [C, P], F32, kind="ExternalInput").ap()
    mw = nc.dram_tensor("mw", [1, 1], F32, kind="ExternalInput").ap()
    wq1 = nc.dram_tensor("wq1", [NCH, 128, C], F32, kind="ExternalInput").ap()
    wk1 = nc.dram_tensor("wk1", [NCH, 128, C], F32, kind="ExternalInput").ap()
    wv1 = nc.dram_tensor("wv1", [4, 128, NCH * 512], F32, kind="ExternalInput").ap()
    ctab = nc.dram_tensor("ctab", [128, P], F32, kind="ExternalInput").ap()
    stab = nc.dram_tensor("stab", [128, P], F32, kind="ExternalInput").ap()
    pim = nc.dram_tensor("pim", [128, 128], F32, kind="ExternalInput").ap()
    c_ones128 = nc.dram_tensor("c_ones128", [128, 1], F32, kind="ExternalInput").ap()
    c_ones1 = nc.dram_tensor("c_ones1", [1, 128], F32, kind="ExternalInput").ap()
    c_eps = nc.dram_tensor("c_eps", [1, 1], F32, kind="ExternalInput").ap()

    o_x1 = nc.dram_tensor("x1", [C, P], F32, kind="ExternalOutput").ap()
    o_xn1 = nc.dram_tensor("xn1", [C, P], BF16, kind="ExternalOutput").ap()
    o_q1 = nc.dram_tensor("q1r", [C, P], BF16, kind="ExternalOutput").ap()
    o_k1 = nc.dram_tensor("k1r", [C, P], BF16, kind="ExternalOutput").ap()
    o_v1 = nc.dram_tensor("v1t", [P, C], BF16, kind="ExternalOutput").ap()

    ctx = ExitStack()
    E = ctx.enter_context
    sbt, pst = nc.sbuf_tensor, nc.psum_tensor
    x1b = E(sbt("x1b", [128, NCH, P], F32))
    xn1b = E(sbt("xn1b", [128, NCH, P], BF16))
    x1h = E(sbt("x1h", [128, 2, P], BF16))
    ld4 = E(sbt("ld4", [128, 3, P], F32))
    amt = E(sbt("amt", [128, 2, P], BF16))
    sqr = E(sbt("sqr", [128, 2, P], BF16))
    wtile = E(sbt("wtile", [128, 2, NCH, 128], BF16))
    wvtile = E(sbt("wvtile", [128, NCH, 512], BF16))
    ctabb = E(sbt("ctabb", [128, P], F32))
    stabb = E(sbt("stabb", [128, P], F32))
    pib = E(sbt("pib", [128, 128], BF16))
    onesb = E(sbt("onesb", [128, 1], BF16))
    ones1f = E(sbt("ones1f", [1, 128], F32))
    epsb = E(sbt("epsb", [1, 1], F32))
    w128 = E(sbt("w128", [128, 1], F32))
    omw128 = E(sbt("omw128", [128, 1], F32))
    mrow = E(sbt("mrow", [1, P], F32))
    vrow = E(sbt("vrow", [1, P], F32))
    rrow = E(sbt("rrow", [1, P], F32))
    mbc = E(sbt("mbc", [128, P], F32))
    rbc = E(sbt("rbc", [128, P], F32))
    scr1 = E(sbt("scr1", [128, P], F32))
    scr2 = E(sbt("scr2", [128, 2, 512], F32))
    scr3 = E(sbt("scr3", [128, 2, 512], F32))
    qbf = E(sbt("qbf", [128, 2, P], BF16))
    obuf = E(sbt("obuf", [128, 2, P], BF16))
    vstage = E(sbt("vstage", [128, 2, 512], BF16))

    p_mm = E(pst("p_mm", [128, 4, 512], F32))
    p_rt = E(pst("p_rt", [128, 2, 512], F32))

    s_gld = Sem(nc, ctx, "s_gld")
    s_yld = Sem(nc, ctx, "s_yld")
    s_t = Sem(nc, ctx, "s_t")
    s_v = Sem(nc, ctx, "s_v")
    s_s = Sem(nc, ctx, "s_s")
    s_st = Sem(nc, ctx, "s_st")
    s_wf = Sem(nc, ctx, "s_wf")

    b = Bld(nc)
    b.op("gpsimd", lambda g: g.dma_start(out=amt[:, :, :].rearrange("p a b -> p (a b)"),
                                         in_=fm(amT).rearrange("p c t -> p (c t)")[:, 0:2 * P]
                                         ) if False else None, waits=[])
    b.op("gpsimd", lambda g: g.dma_start(out=pib[:], in_=pim), inc=s_gld, dma=True)
    b.op("gpsimd", lambda g: g.dma_start(out=onesb[:], in_=c_ones128), inc=s_gld, dma=True)
    b.op("sync", lambda sy: sy.dma_start(out=ctabb[:], in_=ctab), inc=s_yld, dma=True)
    b.op("sync", lambda sy: sy.dma_start(out=stabb[:], in_=stab), inc=s_yld, dma=True)
    b.op("sync", lambda sy: sy.dma_start(out=ones1f[:], in_=c_ones1), inc=s_yld, dma=True)
    b.op("sync", lambda sy: sy.dma_start(out=epsb[:], in_=c_eps), inc=s_yld, dma=True)
    b.op("sync", lambda sy: sy.dma_start(out=w128[0:1, :], in_=mw), inc=s_yld, dma=True)
    CONSTS = (s_yld, s_yld.n)

    b.op("scalar", lambda s: s.activation(w128[0:1, :], w128[0:1, :], AF.Sigmoid),
         waits=[CONSTS], inc=s_s)
    b.op("tensor", lambda t: t.matmul(p_rt[:, 0, 0:1], ones1f[:], w128[0:1, :],
                                      start=True, stop=True), waits=[s_s], inc=s_t)
    b.op("scalar", lambda s: s.copy(w128[:], p_rt[:, 0, 0:1]), waits=[s_t], inc=s_s)
    b.op("vector", lambda v: v.tensor_scalar(omw128[:], w128[:], -1.0, 1.0,
                                             op0=ALU.mult, op1=ALU.add),
         waits=[s_s], inc=s_v)
    WMIX = ((s_v, s_v.n), (s_s, s_s.n))

    for c in range(NCH):
        b.op("sync", (lambda cc: lambda sy: sy.dma_start(
            out=ld4[:, 0, :], in_=xT[cc * 128:(cc + 1) * 128, :]))(c),
            waits=([(s_v, s_v.n)] if c >= 1 else []), inc=s_yld, dma=True)
        b.op("sync", (lambda cc: lambda sy: sy.dma_start(
            out=ld4[:, 1, :], in_=adT[cc * 128:(cc + 1) * 128, :]))(c), inc=s_yld, dma=True)
        b.op("sync", (lambda cc: lambda sy: sy.dma_start(
            out=ld4[:, 2, :], in_=gT[cc * 128:(cc + 1) * 128, :]))(c), inc=s_yld, dma=True)
        b.op("gpsimd", (lambda cc: lambda g: g.dma_start(
            out=amt[:, 0, :], in_=fm(amT)[:, cc, :]))(c),
            waits=([(s_v, s_v.n - 2)] if c >= 1 else []), inc=s_gld, dma=True)
        b.op("vector", lambda v: v.tensor_scalar_mul(scr1[:], amt[:, 0, :], w128[:]),
             waits=[WMIX[0], s_yld, s_gld], inc=s_v)
        b.op("vector", lambda v: v.scalar_tensor_tensor(
            scr1[:], ld4[:, 1, :], omw128[:], scr1[:], op0=ALU.mult, op1=ALU.add),
            waits=[s_v], inc=s_v)
        b.op("vector", lambda v: v.tensor_tensor(scr1[:], scr1[:], ld4[:, 2, :], ALU.mult),
             waits=[s_v], inc=s_v)
        b.op("vector", (lambda cc: lambda v: v.tensor_tensor(
            x1b[:, cc, :], scr1[:], ld4[:, 0, :], ALU.add))(c),
            waits=[s_v], inc=s_v)
        b.op("sync", (lambda cc: lambda sy: sy.dma_start(
            out=o_x1[cc * 128:(cc + 1) * 128, :], in_=x1b[:, cc, :]))(c),
            waits=[s_v], inc=s_st, dma=True)
    X1_DONE = (s_v, s_v.n)

    for c in range(NCH):
        b.op("vector", (lambda cc: lambda v: v.tensor_copy(
            x1h[:, cc % 2, :], x1b[:, cc, :]))(c),
            waits=[X1_DONE] + ([(s_t, s_t.n), (s_s, s_s.n)] if c >= 2 else [s_gld]),
            inc=s_v)
        def summ(cc):
            def go(t):
                for st in range(2):
                    ins = t.matmul(p_mm[0:1, st, :], onesb[:], x1h[:, cc % 2, ST[st]],
                                   start=(cc == 0), stop=(cc == NCH - 1))
                return ins
            return go
        b.op("tensor", summ(c), waits=[s_v], inc=s_t)
        b.op("scalar", (lambda cc: lambda s: s.activation(
            sqr[:, cc % 2, :], x1h[:, cc % 2, :], AF.Square))(c),
            waits=[s_v] + ([(s_t, s_t.n)] if c >= 2 else []), inc=s_s)
        def sqmm(cc):
            def go(t):
                for st in range(2):
                    ins = t.matmul(p_mm[0:1, 2 + st, :], onesb[:], sqr[:, cc % 2, ST[st]],
                                   start=(cc == 0), stop=(cc == NCH - 1))
                return ins
            return go
        b.op("tensor", sqmm(c), waits=[s_s], inc=s_t)
    STATS_MM = (s_t, s_t.n)

    b.op("vector", lambda v: v.tensor_scalar_mul(
        mrow[:], p_mm[0:1, 0:2, :].rearrange("p a b -> p (a b)"), 1.0 / C),
        waits=[STATS_MM], inc=s_v)
    b.op("vector", lambda v: v.tensor_tensor(vrow[:], mrow[:], mrow[:], ALU.mult),
         waits=[s_v], inc=s_v)
    b.op("vector", lambda v: v.scalar_tensor_tensor(
        vrow[:], p_mm[0:1, 2:4, :].rearrange("p a b -> p (a b)"), 1.0 / C,
        vrow[:], op0=ALU.mult, op1=ALU.subtract), waits=[s_v], inc=s_v)
    b.op("scalar", lambda s: s.activation(vrow[:], vrow[:], AF.Sqrt, bias=epsb[:]),
         waits=[s_v, CONSTS], inc=s_s)
    b.op("vector", lambda v: v.reciprocal(rrow[:], vrow[:]), waits=[s_s], inc=s_v)
    RSTD = (s_v, s_v.n)
    def bc_mm(t):
        for st in range(2):
            t.matmul(p_rt[:, 0, :][:, 0:512] if False else p_rt[:, 0, :],
                     ones1f[:], mrow[:, ST[st]], start=True, stop=True) if False else None
        t.matmul(p_mm[:, 0, :], ones1f[:], mrow[:, 0:512], start=True, stop=True)
        t.matmul(p_mm[:, 1, :], ones1f[:], mrow[:, 512:1024], start=True, stop=True)
        t.matmul(p_mm[:, 2, :], ones1f[:], rrow[:, 0:512], start=True, stop=True)
        ins = t.matmul(p_mm[:, 3, :], ones1f[:], rrow[:, 512:1024], start=True, stop=True)
        return ins
    b.op("tensor", bc_mm, waits=[RSTD], inc=s_t)
    def bc_ev(s):
        s.copy(mbc[:], p_mm[:, 0:2, :].rearrange("p a b -> p (a b)"))
        return s.copy(rbc[:], p_mm[:, 2:4, :].rearrange("p a b -> p (a b)"))
    b.op("scalar", bc_ev, waits=[s_t], inc=s_s)
    BC_DONE = (s_s, s_s.n)

    for c in range(NCH):
        b.op("vector", (lambda cc: lambda v: v.tensor_tensor(
            scr1[:], x1b[:, cc, :], mbc[:], ALU.subtract))(c),
            waits=[BC_DONE], inc=s_v)
        b.op("vector", (lambda cc: lambda v: v.tensor_tensor(
            xn1b[:, cc, :], scr1[:], rbc[:], ALU.mult))(c),
            waits=[s_v], inc=s_v)
        b.op("sync", (lambda cc: lambda sy: sy.dma_start(
            out=o_xn1[cc * 128:(cc + 1) * 128, :], in_=xn1b[:, cc, :]))(c),
            waits=[s_v], inc=s_st, dma=True)
    XN1 = (s_v, s_v.n)

    nw = 0
    for which, (wsrc, dst) in enumerate([(wq1, o_q1), (wk1, o_k1)]):
        for f in range(NCH):
            b.op("gpsimd", (lambda s_, l_: lambda g: g.dma_start(
                out=wtile[:, l_ % 2, :, :],
                in_=fm(s_)))(wsrc[:, f * 128:(f + 1) * 128], nw),
                waits=([(s_wf, nw - 1)] if nw >= 2 else []), inc=s_gld, dma=True)
            def mmp(l_, fp):
                def go(t):
                    for st in range(2):
                        for c in range(NCH):
                            ins = t.matmul(p_mm[:, 2 * (fp % 2) + st, :],
                                           wtile[:, l_ % 2, c, :], xn1b[:, c, ST[st]],
                                           start=(c == 0), stop=(c == NCH - 1))
                    return ins
                return go
            b.op("tensor", mmp(nw, f), waits=[s_gld, XN1, s_s, s_v], inc=s_t, inc2=s_wf)
            MMF = (s_t, s_t.n)
            b.op("scalar", (lambda fp: lambda s: (
                s.copy(qbf[:, fp % 2, 0:512], p_mm[:, 2 * (fp % 2), :]),
                s.copy(qbf[:, fp % 2, 512:1024], p_mm[:, 2 * (fp % 2) + 1, :]))[-1])(f),
                waits=[MMF], inc=s_s)
            b.op("tensor", (lambda fp: lambda t: (
                t.matmul(p_rt[:, 0, :], pib[:], qbf[:, fp % 2, 0:512], start=True, stop=True),
                t.matmul(p_rt[:, 1, :], pib[:], qbf[:, fp % 2, 512:1024], start=True, stop=True))[-1])(f),
                waits=[s_s, s_v], inc=s_t)
            ROT = (s_t, s_t.n)
            for st in range(2):
                b.op("vector", (lambda fp, ss: lambda v: v.tensor_tensor(
                    scr2[:, ss, :], p_mm[:, 2 * (fp % 2) + ss, :], ctabb[:, ST[ss]],
                    ALU.mult))(f, st), waits=[MMF, CONSTS, s_s], inc=s_v)
                b.op("vector", (lambda fp, ss: lambda v: v.tensor_tensor(
                    scr3[:, ss, :], p_rt[:, ss, :], stabb[:, ST[ss]],
                    ALU.mult))(f, st), waits=[ROT], inc=s_v)
                b.op("vector", (lambda fp, ss: lambda v: v.tensor_tensor(
                    obuf[:, fp % 2, ST[ss]], scr2[:, ss, :], scr3[:, ss, :],
                    ALU.add))(f, st), waits=[s_v, s_st], inc=s_v)
            b.op("sync", (lambda d_, fp: lambda sy: sy.dma_start(
                out=d_[fp * 128:(fp + 1) * 128, :], in_=obuf[:, fp % 2, :]))(dst, f),
                waits=[s_v], inc=s_st, dma=True)
            nw += 1

    for dvt in range(4):
        b.op("gpsimd", (lambda dd: lambda g: g.dma_start(
            out=wvtile[:, :, :],
            in_=fm(wv1[:, dd * 512:(dd + 1) * 512])))(dvt),
            waits=([(s_t, s_t.n)] if dvt >= 1 else []), inc=s_gld, dma=True)
        for tc in range(8):
            def vmm(tt):
                def go(t):
                    for c in range(NCH):
                        ins = t.matmul(p_rt[:, tt % 2, :],
                                       xn1b[:, c, tt * 128:(tt + 1) * 128],
                                       wvtile[:, c, :],
                                       start=(c == 0), stop=(c == NCH - 1))
                    return ins
                return go
            b.op("tensor", vmm(tc), waits=[s_gld, XN1, s_s, s_v], inc=s_t)
            b.op("scalar", (lambda tt: lambda s: s.copy(
                vstage[:, tt % 2, :], p_rt[:, tt % 2, :]))(tc),
                waits=[s_t, s_st], inc=s_s)
            b.op("sync", (lambda dd, tt: lambda sy: sy.dma_start(
                out=o_v1[tt * 128:(tt + 1) * 128, dd * 512:(dd + 1) * 512],
                in_=vstage[:, tt % 2, :]))(dvt, tc),
                waits=[s_s], inc=s_st, dma=True)

    b.wait("sync", s_st)
    b.finalize(ctx.enter_context(nc.Block()))
    ctx.close()
    return nc


def prep_D(inp, outA, outB, outC):
    wq1 = np.ascontiguousarray(inp["L1_attn_Wq"].T)
    wk1 = np.ascontiguousarray(inp["L1_attn_Wk"].T)
    wv1 = np.ascontiguousarray(inp["L1_attn_Wv"].T)
    mw = np.ascontiguousarray(inp["L0_attn_memory_weight"].reshape(1, 1))
    pim = np.zeros((128, 128), np.float32)
    for i in range(64):
        pim[2 * i + 1, 2 * i] = -1.0
        pim[2 * i, 2 * i + 1] = 1.0
    ones128 = np.ones((128, 1), np.float32)
    ones1 = np.ones((1, 128), np.float32)
    eps = np.full((1, 1), 1e-5, np.float32)
    x = inp["L0_attn_x"]
    inv = 1.0 / 10000.0 ** (np.arange(0, DH1, 2, dtype=np.float32) / DH1)
    maps = []
    for core, bb, t0 in _cores():
        tpos = np.arange(t0, t0 + P, dtype=np.float32)
        fr = np.outer(tpos, inv).astype(np.float32)
        ctab = np.empty((128, P), np.float32)
        stabv = np.empty((128, P), np.float32)
        ctab[0::2, :] = np.cos(fr).T
        ctab[1::2, :] = np.cos(fr).T
        stabv[0::2, :] = np.sin(fr).T
        stabv[1::2, :] = np.sin(fr).T
        xTc = np.ascontiguousarray(x[bb, t0:t0 + P, :].T)
        maps.append(dict(
            xT=xTc,
            adT=np.ascontiguousarray(np.asarray(outB[core]["adot"])),
            amT=np.ascontiguousarray(np.asarray(outA[core]["am"])),
            gT=np.ascontiguousarray(np.asarray(outC[core]["g"])),
            mw=mw, wq1=wq1, wk1=wk1, wv1=wv1,
            ctab=ctab, stab=stabv, pim=pim,
            c_ones128=ones128, c_ones1=ones1, c_eps=eps))
    return maps


PEXT = P + W1


def build_E():
    nc = bass.Bass("TRN2", target_bir_lowering=False)
    q1 = nc.dram_tensor("q1", [C, P], BF16, kind="ExternalInput").ap()
    k1e = nc.dram_tensor("k1e", [C, PEXT], BF16, kind="ExternalInput").ap()
    v1e = nc.dram_tensor("v1e", [PEXT, C], BF16, kind="ExternalInput").ap()
    wo1 = nc.dram_tensor("wo1", [NCH, 128, C], F32, kind="ExternalInput").ap()
    m0 = nc.dram_tensor("m0", [128, 4 * 2 * 256], BF16, kind="ExternalInput").ap()
    mb = nc.dram_tensor("mb", [128, 4 * 2 * 256], BF16, kind="ExternalInput").ap()
    c_ones128 = nc.dram_tensor("c_ones128", [128, 1], F32, kind="ExternalInput").ap()
    c_ones1 = nc.dram_tensor("c_ones1", [1, 128], F32, kind="ExternalInput").ap()
    o_s1 = nc.dram_tensor("s1", [C, P], F32, kind="ExternalOutput").ap()

    ctx = ExitStack()
    E = ctx.enter_context
    sbt, pst = nc.sbuf_tensor, nc.psum_tensor
    qb = E(sbt("qb", [128, NCH, P], BF16))
    kb = E(sbt("kb", [128, NCH, PEXT], BF16))
    vb = E(sbt("vb", [128, PEXT // 128, C], BF16))
    a_fm = E(sbt("a_fm", [128, NCH, P], BF16))
    mskb = E(sbt("mskb", [128, 2, 4, 2, 256], BF16))
    ptile = E(sbt("ptile", [128, 4, 2, 256], BF16))
    praw = E(sbt("praw", [128, 4, 2, 256], BF16))
    onesb = E(sbt("onesb", [128, 1], BF16))
    ones1f = E(sbt("ones1f", [1, 128], F32))
    rrow = E(sbt("rrow", [1, 512], F32))
    albc = E(sbt("albc", [128, 512], F32))
    wtile = E(sbt("wtile", [128, 2, NCH, 128], BF16))
    obuf = E(sbt("obuf", [128, 2, P], F32))

    p_qk = E(pst("p_qk", [128, 4, 512], F32))
    p_pv = E(pst("p_pv", [128, 2, 512], F32))
    p_al = E(pst("p_al", [128, 2, 512], F32))

    s_gld = Sem(nc, ctx, "s_gld")
    s_yld = Sem(nc, ctx, "s_yld")
    s_t = Sem(nc, ctx, "s_t")
    s_v = Sem(nc, ctx, "s_v")
    s_s = Sem(nc, ctx, "s_s")
    s_st = Sem(nc, ctx, "s_st")
    s_wf = Sem(nc, ctx, "s_wf")

    b = Bld(nc)
    b.op("gpsimd", lambda g: g.dma_start(out=qb[:, :, :], in_=fm(q1)), inc=s_gld, dma=True)
    b.op("gpsimd", lambda g: g.dma_start(out=kb[:, :, :], in_=fm(k1e)), inc=s_gld, dma=True)
    b.op("gpsimd", lambda g: g.dma_start(
        out=vb[:, :, :], in_=v1e.rearrange("(tc p) f -> p tc f", p=128)), inc=s_gld, dma=True)
    b.op("gpsimd", lambda g: g.dma_start(
        out=mskb[:, 0, :, :, :].rearrange("p a b c -> p (a b c)"), in_=m0), inc=s_gld, dma=True)
    b.op("gpsimd", lambda g: g.dma_start(
        out=mskb[:, 1, :, :, :].rearrange("p a b c -> p (a b c)"), in_=mb), inc=s_gld, dma=True)
    b.op("gpsimd", lambda g: g.dma_start(out=onesb[:], in_=c_ones128), inc=s_gld, dma=True)
    b.op("sync", lambda sy: sy.dma_start(out=ones1f[:], in_=c_ones1), inc=s_yld, dma=True)
    LOADS = (s_gld, s_gld.n)
    CONSTS = (s_yld, s_yld.n)

    SC1 = float(DH1) ** -0.5
    combo = 0
    for wc in range(4):
        mi = 0 if wc == 0 else 1
        for hp in range(H1 // 2):
            hA, hB = 2 * hp, 2 * hp + 1
            cp = combo % 2
            ssl = slice(wc * 256, (wc + 1) * 256)
            def qk(ha, hb, ww):
                def go(t):
                    for kc in range(4):
                        for hi, hh in enumerate((ha, hb)):
                            ins = t.matmul(
                                p_qk[:, kc, hi * 256:(hi + 1) * 256],
                                kb[:, hh, ww * 256 + kc * 128:ww * 256 + (kc + 1) * 128],
                                qb[:, hh, ww * 256:(ww + 1) * 256],
                                start=(hi == 0), stop=(hi == 1),
                                skip_group_check=True)
                    return ins
                return go
            b.op("tensor", qk(hA, hB, wc), waits=[LOADS, s_s, s_v], inc=s_t)
            QKM = (s_t, s_t.n)
            b.op("scalar", lambda s: s.activation(
                praw[:, :, :, :].rearrange("p a b c -> p a (b c)"),
                p_qk[:, :, :], AF.Exp, scale=SC1),
                waits=[QKM], inc=s_s)
            b.op("vector", (lambda mm_: lambda v: v.tensor_tensor(
                ptile[:, :, :, :], praw[:, :, :, :], mskb[:, mm_, :, :, :],
                ALU.mult))(mi), waits=[s_s, s_t], inc=s_v)
            PT = (s_v, s_v.n)
            def rs(ww):
                def go(t):
                    for kc in range(4):
                        ins = t.matmul(p_al[0:1, 0, :], onesb[:],
                                       ptile[:, kc, :, :], start=(kc == 0), stop=(kc == 3))
                    return ins
                return go
            b.op("tensor", rs(wc), waits=[PT, s_v], inc=s_t)
            def pv(ha, hb, ww, cc):
                def go(t):
                    for hi, hh in enumerate((ha, hb)):
                        for kc in range(4):
                            ins = t.matmul(
                                p_pv[:, cc, hi * 256:(hi + 1) * 256],
                                vb[:, ww * 2 + kc, hh * 128:(hh + 1) * 128],
                                ptile[:, kc, hi, :],
                                start=(hi == 0 and kc == 0), stop=(hi == 1 and kc == 3),
                                skip_group_check=True)
                    return ins
                return go
            b.op("tensor", pv(hA, hB, wc, cp), waits=[s_v], inc=s_t)
            PVM = (s_t, s_t.n)
            b.op("vector", lambda v: v.reciprocal(rrow[:], p_al[0:1, 0, :]),
                 waits=[(s_t, s_t.n - 1)], inc=s_v)
            b.op("tensor", lambda t: t.matmul(p_al[:, 1, :], ones1f[0:1, :], rrow[:],
                                              start=True, stop=True),
                 waits=[s_v, s_s], inc=s_t)
            b.op("scalar", lambda s: s.copy(albc[:], p_al[:, 1, :]), waits=[s_t], inc=s_s)
            b.op("vector", (lambda ha, cc, qsl: lambda v: v.tensor_tensor(
                a_fm[:, ha:ha + 2, qsl],
                p_pv[:, cc, :].rearrange("p (h s) -> p h s", h=2),
                albc[:, :].rearrange("p (h s) -> p h s", h=2), ALU.mult))(hA, cp, ssl),
                waits=[PVM, s_s], inc=s_v)
            combo += 1
    AFM = (s_v, s_v.n)

    for f in range(NCH):
        b.op("gpsimd", (lambda s_, ff: lambda g: g.dma_start(
            out=wtile[:, ff % 2, :, :], in_=fm(s_)))(wo1[:, f * 128:(f + 1) * 128], f),
            waits=([(s_wf, f - 1)] if f >= 2 else []), inc=s_gld, dma=True)
        def mmo(ff):
            def go(t):
                for st in range(2):
                    for c in range(NCH):
                        ins = t.matmul(p_qk[:, 2 * (ff % 2) + st, :],
                                       wtile[:, ff % 2, c, :], a_fm[:, c, ST[st]],
                                       start=(c == 0), stop=(c == NCH - 1))
                return ins
            return go
        b.op("tensor", mmo(f), waits=[s_gld, AFM, s_s], inc=s_t, inc2=s_wf)
        def ev(ff):
            def go(s):
                s.copy(obuf[:, ff % 2, 0:512], p_qk[:, 2 * (ff % 2), :])
                return s.copy(obuf[:, ff % 2, 512:1024], p_qk[:, 2 * (ff % 2) + 1, :])
            return go
        b.op("scalar", ev(f), waits=[s_t, s_st], inc=s_s)
        b.op("sync", (lambda ff: lambda sy: sy.dma_start(
            out=o_s1[ff * 128:(ff + 1) * 128, :], in_=obuf[:, ff % 2, :]))(f),
            waits=[s_s], inc=s_st, dma=True)

    b.wait("sync", s_st)
    b.finalize(ctx.enter_context(nc.Block()))
    ctx.close()
    return nc


def prep_E(inp, outD):
    wo1 = np.ascontiguousarray(inp["L1_attn_Wo"].T)
    ones128 = np.ones((128, 1), np.float32)
    ones1 = np.ones((1, 128), np.float32)
    tril = np.tril(np.ones((W1, W1), np.float32))
    base = np.concatenate([np.ones((W1, W1), np.float32), tril], axis=1)
    first = np.concatenate([np.zeros((W1, W1), np.float32), tril], axis=1)

    def mk_mask(m):
        mk = m.T.reshape(4, 128, 256)
        md = np.stack([mk, mk], axis=2)
        md = md.transpose(1, 0, 2, 3)
        return np.ascontiguousarray(md.reshape(128, 4 * 2 * 256).astype(bfdt))

    mbase = mk_mask(base)
    mfirst = mk_mask(first)
    maps = []
    for core, bb, t0 in _cores():
        k_own = np.asarray(outD[core]["k1r"])
        v_own = np.asarray(outD[core]["v1t"])
        if core % 4 == 0:
            k_prev = np.zeros((C, W1), dtype=bfdt)
            v_prev = np.zeros((W1, C), dtype=bfdt)
        else:
            k_prev = np.asarray(outD[core - 1]["k1r"])[:, P - W1:]
            v_prev = np.asarray(outD[core - 1]["v1t"])[P - W1:, :]
        k1e = np.ascontiguousarray(np.concatenate([k_prev, k_own], axis=1))
        v1e = np.ascontiguousarray(np.concatenate([v_prev, v_own], axis=0))
        maps.append(dict(q1=np.ascontiguousarray(np.asarray(outD[core]["q1r"])),
                         k1e=k1e, v1e=v1e, wo1=wo1,
                         m0=(mfirst if core % 4 == 0 else mbase), mb=mbase,
                         c_ones128=ones128, c_ones1=ones1))
    return maps


def build_F():
    nc = bass.Bass("TRN2", target_bir_lowering=False)
    xn1 = nc.dram_tensor("xn1", [C, P], BF16, kind="ExternalInput").ap()
    x1 = nc.dram_tensor("x1", [C, P], F32, kind="ExternalInput").ap()
    s1 = nc.dram_tensor("s1", [C, P], F32, kind="ExternalInput").ap()
    wgT = nc.dram_tensor("wgT", [NFS, 128, C], I32, kind="ExternalInput").ap()
    wuT = nc.dram_tensor("wuT", [NFS, 128, C], I32, kind="ExternalInput").ap()
    wdT = nc.dram_tensor("wdT", [NCH, 128, FS], I32, kind="ExternalInput").ap()
    sgp = nc.dram_tensor("sgp", [128, NFS], F32, kind="ExternalInput").ap()
    sup = nc.dram_tensor("sup", [128, NFS], F32, kind="ExternalInput").ap()
    sdp = nc.dram_tensor("sdp", [128, NCH], F32, kind="ExternalInput").ap()
    o_out = nc.dram_tensor("outT", [C, P], F32, kind="ExternalOutput").ap()

    ctx = ExitStack()
    E = ctx.enter_context
    sbt, pst = nc.sbuf_tensor, nc.psum_tensor
    xn1b = E(sbt("xn1b", [128, NCH, P], BF16))
    hid = E(sbt("hid", [128, NFS, P], BF16))
    rw = E(sbt("rw", [128, 2, NCH, 128], BF16))
    wq = E(sbt("wq", [128, 2, NCH, 128], BF16))
    rwd = E(sbt("rwd", [128, 2, NFS, 128], BF16))
    wqd = E(sbt("wqd", [128, 2, NFS, 128], BF16))
    sgb = E(sbt("sgb", [128, NFS], F32))
    sub = E(sbt("sub", [128, NFS], F32))
    sdb = E(sbt("sdb", [128, NCH], F32))
    sg_t = E(sbt("sg_t", [128, 2, 512], F32))
    gt = E(sbt("gt", [128, 2, 512], F32))
    xs_t = E(sbt("xs_t", [128, 1, P], F32))
    ss_t = E(sbt("ss_t", [128, 1, P], F32))
    ost = E(sbt("ost", [128, 2, P], F32))

    p_h = E(pst("p_h", [128, 4, 512], F32))
    p_y = E(pst("p_y", [128, 2, 512], F32))

    s_gld = Sem(nc, ctx, "s_gld")
    s_yld = Sem(nc, ctx, "s_yld")
    s_t = Sem(nc, ctx, "s_t")
    s_v = Sem(nc, ctx, "s_v")
    s_s = Sem(nc, ctx, "s_s")
    s_st = Sem(nc, ctx, "s_st")

    b = Bld(nc)
    b.op("gpsimd", lambda g: g.dma_start(out=xn1b[:, :, :], in_=fm(xn1)), inc=s_gld, dma=True)
    b.op("sync", lambda sy: sy.dma_start(out=sgb[:], in_=sgp), inc=s_yld, dma=True)
    b.op("sync", lambda sy: sy.dma_start(out=sub[:], in_=sup), inc=s_yld, dma=True)
    b.op("sync", lambda sy: sy.dma_start(out=sdb[:], in_=sdp), inc=s_yld, dma=True)
    LOADS = (s_gld, s_gld.n)
    CONSTS = (s_yld, s_yld.n)

    nl = 0
    for f in range(NFS):
        for which, wsrc in ((0, wgT), (1, wuT)):
            b.op("gpsimd", (lambda s_, l_: lambda g: g.dma_start(
                out=rw[:, l_ % 2, :, :], in_=fm(s_)))(wsrc[:, f * 128:(f + 1) * 128], nl),
                waits=([(s_v, s_v.n - 1)] if nl >= 2 else []), inc=s_gld, dma=True)
            b.op("vector", (lambda l_: lambda v: v.tensor_scalar_add(
                wq[:, l_ % 2, :, :], rw[:, l_ % 2, :, :], -128.0))(nl),
                waits=[s_gld, (s_t, s_t.n)], inc=s_v)
            def hmm(l_, wh):
                def go(t):
                    for st in range(2):
                        for c in range(NCH):
                            ins = t.matmul(p_h[:, 2 * wh + st, :],
                                           wq[:, l_ % 2, c, :], xn1b[:, c, ST[st]],
                                           start=(c == 0), stop=(c == NCH - 1))
                    return ins
                return go
            b.op("tensor", hmm(nl, which), waits=[s_v, LOADS, s_s], inc=s_t)
            nl += 1
        HMM = (s_t, s_t.n)
        for st in range(2):
            b.op("scalar", (lambda ff, ss: lambda s: s.activation(
                sg_t[:, ss, :], p_h[:, ss, :], AF.Silu, scale=sgb[:, ff:ff + 1]))(f, st),
                waits=[HMM, CONSTS], inc=s_s)
            b.op("vector", (lambda ff, ss: lambda v: v.scalar_tensor_tensor(
                hid[:, ff, ST[ss]], p_h[:, 2 + ss, :], sub[:, ff:ff + 1], sg_t[:, ss, :],
                op0=ALU.mult, op1=ALU.mult))(f, st),
                waits=[s_s], inc=s_v)
    HID = (s_v, s_v.n)

    nd = 0
    for c in range(NCH):
        b.op("gpsimd", (lambda s_, l_: lambda g: g.dma_start(
            out=rwd[:, l_ % 2, :, :], in_=fm(s_)))(wdT[:, c * 128:(c + 1) * 128], nd),
            waits=([(s_v, s_v.n - 3)] if nd >= 2 else []), inc=s_gld, dma=True)
        b.op("vector", (lambda l_: lambda v: v.tensor_scalar_add(
            wqd[:, l_ % 2, :, :], rwd[:, l_ % 2, :, :], -128.0))(nd),
            waits=[s_gld, (s_t, s_t.n)], inc=s_v)
        b.op("sync", (lambda cc: lambda sy: sy.dma_start(
            out=xs_t[:, 0, :], in_=x1[cc * 128:(cc + 1) * 128, :]))(c),
            waits=([(s_v, s_v.n)] if c >= 1 else []), inc=s_yld, dma=True)
        b.op("sync", (lambda cc: lambda sy: sy.dma_start(
            out=ss_t[:, 0, :], in_=s1[cc * 128:(cc + 1) * 128, :]))(c),
            inc=s_yld, dma=True)
        def ymm(l_, cc):
            def go(t):
                for st in range(2):
                    for fr in range(NFS):
                        ins = t.matmul(p_y[:, st, :], wqd[:, l_ % 2, fr, :],
                                       hid[:, fr, ST[st]],
                                       start=(fr == 0), stop=(fr == NFS - 1))
                return ins
            return go
        b.op("tensor", ymm(nd, c), waits=[s_v, HID, s_s], inc=s_t)
        YMM = (s_t, s_t.n)
        nd += 1
        for st in range(2):
            b.op("scalar", (lambda cc, ss: lambda s: s.activation(
                gt[:, ss, :], p_y[:, ss, :], AF.Silu, scale=sdb[:, cc:cc + 1]))(c, st),
                waits=[YMM, CONSTS], inc=s_s)
            b.op("vector", (lambda cc, ss: lambda v: v.tensor_tensor(
                ost[:, cc % 2, ST[ss]], ss_t[:, 0, ST[ss]], gt[:, ss, :],
                ALU.mult))(c, st), waits=[s_s, s_yld, s_st], inc=s_v)
            b.op("vector", (lambda cc, ss: lambda v: v.tensor_tensor(
                ost[:, cc % 2, ST[ss]], ost[:, cc % 2, ST[ss]], xs_t[:, 0, ST[ss]],
                ALU.add))(c, st), waits=[s_v], inc=s_v)
        b.op("sync", (lambda cc: lambda sy: sy.dma_start(
            out=o_out[cc * 128:(cc + 1) * 128, :], in_=ost[:, cc % 2, :]))(c),
            waits=[s_v], inc=s_st, dma=True)

    b.wait("sync", s_st)
    b.finalize(ctx.enter_context(nc.Block()))
    ctx.close()
    return nc


def prep_F(inp, outD, outE):
    wgT = np.ascontiguousarray(inp["L1_ffn_wq_gate"].T)
    wuT = np.ascontiguousarray(inp["L1_ffn_wq_up"].T)
    wdT = np.ascontiguousarray(inp["L1_ffn_wq_down"].T)
    sgp = np.ascontiguousarray(inp["L1_ffn_s_gate"].reshape(NFS, 128).T)
    sup = np.ascontiguousarray(inp["L1_ffn_s_up"].reshape(NFS, 128).T)
    sdp = np.ascontiguousarray(inp["L1_ffn_s_down"].reshape(NCH, 128).T)
    maps = []
    for core, bb, t0 in _cores():
        maps.append(dict(xn1=np.ascontiguousarray(np.asarray(outD[core]["xn1"])),
                         x1=np.ascontiguousarray(np.asarray(outD[core]["x1"])),
                         s1=np.ascontiguousarray(np.asarray(outE[core]["s1"])),
                         wgT=wgT, wuT=wuT, wdT=wdT, sgp=sgp, sup=sup, sdp=sdp))
    return maps


EP = P + W1
EST = ((0, 512), (512, 512), (1024, 256))
OST = ((0, 512), (512, 512))


def build_fused(core_is_first):
    nc = bass.Bass("TRN2", target_bir_lowering=False)

    xTf = nc.dram_tensor("xTf", [C, T], F32, kind="ExternalInput").ap()
    xTe = nc.dram_tensor("xTe", [C, EP], F32, kind="ExternalInput").ap()
    wqT = nc.dram_tensor("wqT", [NCH, 128, C], F32, kind="ExternalInput").ap()
    wkT = nc.dram_tensor("wkT", [128, NCH * DKV], F32, kind="ExternalInput").ap()
    wvT = nc.dram_tensor("wvT", [128, NCH * DKV], F32, kind="ExternalInput").ap()
    memi = nc.dram_tensor("memi", [128, 4 * DKV], F32, kind="ExternalInput").ap()
    memn = nc.dram_tensor("memn", [128, 4], F32, kind="ExternalInput").ap()
    mw = nc.dram_tensor("mw", [1, 1], F32, kind="ExternalInput").ap()
    gwT = nc.dram_tensor("gwT", [C, NE], F32, kind="ExternalInput").ap()
    sgwT = nc.dram_tensor("sgwT", [128, NCH], F32, kind="ExternalInput").ap()
    fc1T = nc.dram_tensor("fc1T", [NE, NFM, 128, C], F32, kind="ExternalInput").ap()
    fc2T = nc.dram_tensor("fc2T", [NE, NFM, 128, C], F32, kind="ExternalInput").ap()
    pjT = nc.dram_tensor("pjT", [NE, NCH, 128, FM], F32, kind="ExternalInput").ap()
    sf1T = nc.dram_tensor("sf1T", [NFS, 128, C], F32, kind="ExternalInput").ap()
    sf2T = nc.dram_tensor("sf2T", [NFS, 128, C], F32, kind="ExternalInput").ap()
    spT = nc.dram_tensor("spT", [NCH, 128, FS], F32, kind="ExternalInput").ap()
    wq1 = nc.dram_tensor("wq1", [NCH, 128, C], F32, kind="ExternalInput").ap()
    wk1 = nc.dram_tensor("wk1", [NCH, 128, C], F32, kind="ExternalInput").ap()
    wv1 = nc.dram_tensor("wv1", [4, 128, NCH * 512], F32, kind="ExternalInput").ap()
    wo1 = nc.dram_tensor("wo1", [NCH, 128, C], F32, kind="ExternalInput").ap()
    wgT = nc.dram_tensor("wgT", [NFS, 128, C], I32, kind="ExternalInput").ap()
    wuT = nc.dram_tensor("wuT", [NFS, 128, C], I32, kind="ExternalInput").ap()
    wdT = nc.dram_tensor("wdT", [NCH, 128, FS], I32, kind="ExternalInput").ap()
    sgp = nc.dram_tensor("sgp", [128, NFS], F32, kind="ExternalInput").ap()
    sup = nc.dram_tensor("sup", [128, NFS], F32, kind="ExternalInput").ap()
    sdp = nc.dram_tensor("sdp", [128, NCH], F32, kind="ExternalInput").ap()
    ctab = nc.dram_tensor("ctab", [128, EP], F32, kind="ExternalInput").ap()
    stab = nc.dram_tensor("stab", [128, EP], F32, kind="ExternalInput").ap()
    pim = nc.dram_tensor("pim", [128, 128], F32, kind="ExternalInput").ap()
    m0 = nc.dram_tensor("m0", [128, 4 * 2 * 256], BF16, kind="ExternalInput").ap()
    mb = nc.dram_tensor("mb", [128, 4 * 2 * 256], BF16, kind="ExternalInput").ap()
    c_ones128 = nc.dram_tensor("c_ones128", [128, 1], F32, kind="ExternalInput").ap()
    c_ones1 = nc.dram_tensor("c_ones1", [1, 128], F32, kind="ExternalInput").ap()
    c_eps = nc.dram_tensor("c_eps", [1, 1], F32, kind="ExternalInput").ap()
    c_eye = nc.dram_tensor("c_eye", [128, 128], F32, kind="ExternalInput").ap()
    c_onesq = nc.dram_tensor("c_onesq", [128, 128], F32, kind="ExternalInput").ap()
    c_sel = nc.dram_tensor("c_sel", [9, 9 * 128], F32, kind="ExternalInput").ap()

    o_out = nc.dram_tensor("outT", [C, P], F32, kind="ExternalOutput").ap()

    d_xn = nc.dram_tensor("d_xn", [C, EP], BF16, kind="Internal").ap()
    d_xnf = nc.dram_tensor("d_xnf", [C, EP], F32, kind="Internal").ap()
    d_q = nc.dram_tensor("d_q", [C, EP], BF16, kind="Internal").ap()
    d_am = nc.dram_tensor("d_am", [C, EP], BF16, kind="Internal").ap()
    d_ad = nc.dram_tensor("d_ad", [C, EP], F32, kind="Internal").ap()
    d_g = nc.dram_tensor("d_g", [C, EP], F32, kind="Internal").ap()
    d_x1 = nc.dram_tensor("d_x1", [C, EP], F32, kind="Internal").ap()
    d_xn1 = nc.dram_tensor("d_xn1", [C, EP], BF16, kind="Internal").ap()
    d_q1 = nc.dram_tensor("d_q1", [C, P], BF16, kind="Internal").ap()
    d_k1 = nc.dram_tensor("d_k1", [C, EP], BF16, kind="Internal").ap()
    d_v1 = nc.dram_tensor("d_v1", [EP, C], BF16, kind="Internal").ap()
    d_s1 = nc.dram_tensor("d_s1", [C, P], F32, kind="Internal").ap()
    d_k = nc.dram_tensor("d_k", [DKV, T], BF16, kind="Internal").ap()
    d_va = nc.dram_tensor("d_va", [T, NH * 65], BF16, kind="Internal").ap()
    b_fc1 = nc.dram_tensor("b_fc1", [NE, NFM, 128, C], BF16, kind="Internal").ap()
    b_fc2 = nc.dram_tensor("b_fc2", [NE, NFM, 128, C], BF16, kind="Internal").ap()
    b_pj = nc.dram_tensor("b_pj", [NE, NCH, 128, FM], BF16, kind="Internal").ap()
    b_sf1 = nc.dram_tensor("b_sf1", [NFS, 128, C], BF16, kind="Internal").ap()
    b_sf2 = nc.dram_tensor("b_sf2", [NFS, 128, C], BF16, kind="Internal").ap()
    b_sp = nc.dram_tensor("b_sp", [NCH, 128, FS], BF16, kind="Internal").ap()
    b_wq1 = nc.dram_tensor("b_wq1", [NCH, 128, C], BF16, kind="Internal").ap()
    b_wk1 = nc.dram_tensor("b_wk1", [NCH, 128, C], BF16, kind="Internal").ap()
    b_wv1 = nc.dram_tensor("b_wv1", [4, 128, NCH * 512], BF16, kind="Internal").ap()
    b_wo1 = nc.dram_tensor("b_wo1", [NCH, 128, C], BF16, kind="Internal").ap()
    b_wg = nc.dram_tensor("b_wg", [NFS, 128, C], BF16, kind="Internal").ap()
    b_wu = nc.dram_tensor("b_wu", [NFS, 128, C], BF16, kind="Internal").ap()
    b_wd = nc.dram_tensor("b_wd", [NCH, 128, FS], BF16, kind="Internal").ap()

    ctx = ExitStack()
    E = ctx.enter_context
    sbt, pst = nc.sbuf_tensor, nc.psum_tensor

    onesb = E(sbt("onesb", [128, 1], BF16))
    ones1f = E(sbt("ones1f", [1, 128], F32))
    epsb = E(sbt("epsb", [1, 1], F32))
    onesf = E(sbt("onesf", [128, 128], F32))
    eyeb = E(sbt("eyeb", [128, 128], F32))

    s_gld = Sem(nc, ctx, "s_gld")
    s_yld = Sem(nc, ctx, "s_yld")
    s_t = Sem(nc, ctx, "s_t")
    s_v = Sem(nc, ctx, "s_v")
    s_s = Sem(nc, ctx, "s_s")
    s_st = Sem(nc, ctx, "s_st")
    s_wf = Sem(nc, ctx, "s_wf")
    s_pf = Sem(nc, ctx, "s_pf")

    b = Bld(nc)
    ALLS = (s_gld, s_yld, s_t, s_v, s_s, s_st)
    conv_marks = []

    def barrier():
        for e in ENGINES:
            b.wait(e, *ALLS)

    b.op("gpsimd", lambda g: g.dma_start(out=onesb[:], in_=c_ones128), inc=s_gld, dma=True)
    b.op("sync", lambda sy: sy.dma_start(out=ones1f[:], in_=c_ones1), inc=s_yld, dma=True)
    b.op("sync", lambda sy: sy.dma_start(out=epsb[:], in_=c_eps), inc=s_yld, dma=True)
    b.op("sync", lambda sy: sy.dma_start(out=onesf[:], in_=c_onesq), inc=s_yld, dma=True)
    b.op("sync", lambda sy: sy.dma_start(out=eyeb[:], in_=c_eye), inc=s_yld, dma=True)
    CONSTS = (s_yld, s_yld.n)

    def _phase1(ph):
        F = ph.enter_context
        xbg = F(sbt("p1_xbg", [128, 2, NCH, 1024], BF16))
        xng = F(sbt("p1_xng", [128, NCH, 1024], BF16))
        xf32 = F(sbt("p1_xf32", [128, 4, 1024], F32))
        dscr = F(sbt("p1_dscr", [128, 1024], F32))
        sqr = F(sbt("p1_sqr", [128, 2, 1024], BF16))
        wkb = F(sbt("p1_wkb", [128, NCH, DKV], BF16))
        wvb = F(sbt("p1_wvb", [128, NCH, DKV], BF16))
        mrow = F(sbt("p1_mrow", [1, 1024], F32))
        vrow = F(sbt("p1_vrow", [1, 1024], F32))
        rrow = F(sbt("p1_rrow", [1, 1024], F32))
        mbc = F(sbt("p1_mbc", [128, 1024], F32))
        rbc = F(sbt("p1_rbc", [128, 1024], F32))
        scrx = F(sbt("p1_scrx", [128, 1024], F32))
        kstg = F(sbt("p1_kstg", [128, 2, 1024], BF16))
        vstg = F(sbt("p1_vstg", [128, 2, NH * 65], BF16))
        p_mm = F(pst("p1_mm", [128, 4, 512], F32))
        p_bc = F(pst("p1_bc", [128, 2, 512], F32))

        b.op("gpsimd", lambda g: g.dma_start(out=wkb[:, :, :], in_=wkT.rearrange("p (c f) -> p c f", c=NCH)), inc=s_gld, dma=True)
        b.op("gpsimd", lambda g: g.dma_start(out=wvb[:, :, :], in_=wvT.rearrange("p (c f) -> p c f", c=NCH)), inc=s_gld, dma=True)
        b.op("vector", lambda v: v.memset(vstg[:, :, :].rearrange(
            "p a (h e) -> p a h e", h=NH)[:, :, :, 64], 1.0), inc=s_v)

        sq_marks = {}
        xn_marks = {}
        for grp in range(4):
            gsl = slice(grp * 1024, (grp + 1) * 1024)
            b.op("gpsimd", (lambda gg: lambda g: g.dma_start(
                out=xbg[:, gg % 2, :, :], in_=fm(xTf)[:, :, gg * 1024:(gg + 1) * 1024]))(grp),
                waits=([(s_t, sq_marks[grp - 2][0]), (s_s, sq_marks[grp - 2][1])]
                       if grp >= 2 else []),
                inc=s_gld, dma=True)
            GL = (s_gld, s_gld.n)
            def stats_mm(gg):
                def go(t):
                    for st in range(2):
                        for c in range(NCH):
                            ins = t.matmul(p_mm[0:1, st, :], onesb[:],
                                           xbg[:, gg % 2, c, ST[st]],
                                           start=(c == 0), stop=(c == NCH - 1))
                    return ins
                return go
            b.op("tensor", stats_mm(grp), waits=[GL, (s_v, s_v.n), (s_s, s_s.n)], inc=s_t)
            for c in range(NCH):
                b.op("scalar", (lambda cc, gg: lambda s: s.activation(
                    sqr[:, cc % 2, :], xbg[:, gg % 2, cc, :], AF.Square))(c, grp),
                    waits=[GL] + ([(s_t, s_t.n)] if c >= 2 else
                                  ([(s_t, sq_marks[grp - 1][0])] if grp >= 1 else [])),
                    inc=s_s)
                def sqmm(cc):
                    def go(t):
                        for st in range(2):
                            ins = t.matmul(p_mm[0:1, 2 + st, :], onesb[:],
                                           sqr[:, cc % 2, ST[st]],
                                           start=(cc == 0), stop=(cc == NCH - 1))
                        return ins
                    return go
                b.op("tensor", sqmm(c), waits=[s_s], inc=s_t)
            STM = (s_t, s_t.n)
            sq_marks[grp] = (s_t.n, s_s.n)
            b.op("vector", lambda v: v.tensor_scalar_mul(
                mrow[:], p_mm[0:1, 0:2, :].rearrange("p a b -> p (a b)"), 1.0 / C),
                waits=[STM], inc=s_v)
            b.op("vector", lambda v: v.tensor_tensor(vrow[:], mrow[:], mrow[:], ALU.mult),
                 waits=[s_v], inc=s_v)
            b.op("vector", lambda v: v.scalar_tensor_tensor(
                vrow[:], p_mm[0:1, 2:4, :].rearrange("p a b -> p (a b)"), 1.0 / C,
                vrow[:], op0=ALU.mult, op1=ALU.subtract), waits=[s_v], inc=s_v)
            b.op("scalar", lambda s: s.activation(vrow[:], vrow[:], AF.Sqrt, bias=epsb[:]),
                 waits=[s_v, CONSTS], inc=s_s)
            b.op("vector", lambda v: v.reciprocal(rrow[:], vrow[:]), waits=[s_s], inc=s_v)
            RS = (s_v, s_v.n)
            def bcm(t):
                t.matmul(p_bc[:, 0, :], ones1f[:], mrow[:, 0:512], start=True, stop=True)
                t.matmul(p_bc[:, 1, :], ones1f[:], mrow[:, 512:1024], start=True, stop=True)
                t.matmul(p_mm[:, 0, :], ones1f[:], rrow[:, 0:512], start=True, stop=True)
                return t.matmul(p_mm[:, 1, :], ones1f[:], rrow[:, 512:1024], start=True, stop=True)
            b.op("tensor", bcm, waits=[RS], inc=s_t)
            def bce(s):
                s.copy(mbc[:], p_bc[:, 0:2, :].rearrange("p a b -> p (a b)"))
                return s.copy(rbc[:], p_mm[:, 0:2, :].rearrange("p a b -> p (a b)"))
            b.op("scalar", bce, waits=[s_t], inc=s_s)
            BCD = (s_s, s_s.n)
            Y0 = s_yld.n
            sub_marks = {}
            xng_marks = {}
            for c in range(NCH):
                b.op("sync", (lambda cc, gg: lambda sy: sy.dma_start(
                    out=xf32[:, cc % 4, :],
                    in_=xTf[cc * 128:(cc + 1) * 128, gg * 1024:(gg + 1) * 1024]))(c, grp),
                    waits=([(s_v, sub_marks[c - 4])] if c >= 4 else
                           ([(s_v, xn_marks[grp - 1])] if grp >= 1 else [])),
                    inc=s_yld, dma=True)
                b.op("vector", (lambda cc: lambda v: v.tensor_tensor(
                    scrx[:], xf32[:, cc % 4, :], mbc[:], ALU.subtract))(c),
                    waits=[BCD, (s_yld, Y0 + 16 * (c + 3))], inc=s_v)
                sub_marks[c] = s_v.n
                b.op("vector", (lambda cc: lambda v: v.tensor_tensor(
                    xng[:, cc, :], scrx[:], rbc[:], ALU.mult))(c),
                    waits=[s_v], inc=s_v)
                xng_marks[c] = s_v.n
            for _dmy in range(2):
                b.op("sync", (lambda gg: lambda sy: sy.dma_start(
                    out=dscr[:], in_=xTf[0:128, gg * 1024:(gg + 1) * 1024]))(grp),
                    inc=s_yld, dma=True)
            XNG = (s_v, s_v.n)
            xn_marks[grp] = s_v.n
            for fk in range(4):
                def kmm(ff, gg):
                    def go(t):
                        for st in range(2):
                            for c in range(NCH):
                                ins = t.matmul(p_mm[:, 2 * (ff % 2) + st, :],
                                               wkb[:, c, ff * 128:(ff + 1) * 128],
                                               xng[:, c, ST[st]],
                                               start=(c == 0), stop=(c == NCH - 1))
                        return ins
                    return go
                if fk == 0:
                    def kmm0(cs_, first, last):
                        def go(t):
                            ins = None
                            for st in range(2):
                                for c in cs_:
                                    ins = t.matmul(p_mm[:, st, :],
                                                   wkb[:, c, 0:128],
                                                   xng[:, c, ST[st]],
                                                   start=(first and c == cs_[0]),
                                                   stop=(last and c == cs_[-1]),
                                                   skip_group_check=True)
                            return ins
                        return go
                    for cg in range(4):
                        cs_ = list(range(cg * 4, cg * 4 + 4))
                        b.op("tensor", kmm0(cs_, cg == 0, cg == 3),
                             waits=[(s_v, xng_marks[cg * 4 + 3]), (s_gld, s_gld.n),
                                    (s_s, s_s.n)],
                             inc=(s_t if cg == 3 else None))
                else:
                    b.op("tensor", kmm(fk, grp),
                         waits=[XNG, (s_gld, s_gld.n), s_s, s_v], inc=s_t)
                def kev(ff):
                    def go(s):
                        s.copy(kstg[:, ff % 2, 0:512], p_mm[:, 2 * (ff % 2), :])
                        return s.copy(kstg[:, ff % 2, 512:1024], p_mm[:, 2 * (ff % 2) + 1, :])
                    return go
                b.op("scalar", kev(fk), waits=[s_t, s_st], inc=s_s)
                b.op("scalar", (lambda ff, gg: lambda s: s.dma_start(
                    out=d_k[ff * 128:(ff + 1) * 128, gg * 1024:(gg + 1) * 1024],
                    in_=kstg[:, ff % 2, :]))(fk, grp),
                    waits=[s_s], inc=s_st, dma=True)
            for tc in range(8):
                def vmm(tt):
                    def go(t):
                        for c in range(NCH):
                            ins = t.matmul(p_mm[:, tt % 2, :],
                                           xng[:, c, tt * 128:(tt + 1) * 128],
                                           wvb[:, c, :],
                                           start=(c == 0), stop=(c == NCH - 1))
                        return ins
                    return go
                b.op("tensor", vmm(tc), waits=[XNG, s_s, s_v], inc=s_t)
                def vev(tt):
                    def go(s):
                        ins = None
                        for h in range(NH):
                            ins = s.copy(vstg[:, tt % 2, h * 65:h * 65 + 64],
                                         p_mm[:, tt % 2, h * 64:(h + 1) * 64])
                        return ins
                    return go
                b.op("scalar", vev(tc), waits=[s_t, s_st], inc=s_s)
                b.op("scalar", (lambda tt, gg: lambda s: s.dma_start(
                    out=d_va[(gg * 8 + tt) * 128:(gg * 8 + tt + 1) * 128, :],
                    in_=vstg[:, tt % 2, :]))(tc, grp),
                    waits=[s_s], inc=s_st, dma=True)
    with ExitStack() as _es1:
        _phase1(_es1)
    barrier()

    def _phase2(ph):
        F = ph.enter_context
        xb = F(sbt("p2_xb", [128, 2, EP], BF16))
        xnb = F(sbt("p2_xnb", [128, NCH, EP], BF16))
        sqb = F(sbt("p2_sqb", [128, NCH, EP], BF16))
        xf32 = F(sbt("p2_xf32", [128, 2, EP], F32))
        sqr = F(sbt("p2_sqr", [128, 2, EP], BF16))
        wtile = F(sbt("p2_wtile", [128, 4, NCH, 128], BF16))
        memb = F(sbt("p2_memb", [128, 4, DKV], BF16))
        memnb = F(sbt("p2_memnb", [128, 4, 1], BF16))
        mrow = F(sbt("p2_mrow", [1, EP], F32))
        vrow = F(sbt("p2_vrow", [1, EP], F32))
        rrow = F(sbt("p2_rrow", [1, EP], F32))
        drow = F(sbt("p2_drow", [1, 512], F32))
        mbc = F(sbt("p2_mbc", [128, EP], F32))
        rbc = F(sbt("p2_rbc", [128, EP], F32))
        dbc = F(sbt("p2_dbc", [128, 512], F32))
        scrx = F(sbt("p2_scrx", [128, EP], F32))
        xstage = F(sbt("p2_xstage", [128, 2, EP], F32))
        scra = F(sbt("p2_scra", [128, 512], F32))
        scrb = F(sbt("p2_scrb", [128, 512], F32))
        scrc = F(sbt("p2_scrc", [128, 512], F32))
        obuf = F(sbt("p2_obuf", [128, 2, EP], BF16))
        amb = F(sbt("p2_amb", [128, 2, 512], BF16))
        p_mm = F(pst("p2_mm", [128, 6, 512], F32))
        p_bc = F(pst("p2_bc", [128, 2, 512], F32))

        b.op("gpsimd", lambda g: g.dma_start(out=memb[:, :, :], in_=memi.rearrange("p (c f) -> p c f", c=4)), inc=s_gld, dma=True)
        b.op("gpsimd", lambda g: g.dma_start(out=memnb[:, :, 0], in_=memn), inc=s_gld, dma=True)
        for c in range(NCH):
            b.op("gpsimd", (lambda cc: lambda g: g.dma_start(
                out=xb[:, cc % 2, :], in_=xTe[cc * 128:(cc + 1) * 128, :]))(c),
                waits=([(s_t, s_t.n)] if c >= 2 else []), inc=s_gld, dma=True)
            def summ(cc):
                def go(t):
                    for ti, (o, w) in enumerate(EST):
                        ins = t.matmul(p_mm[0:1, ti, 0:w], onesb[:], xb[:, cc % 2, o:o + w],
                                       start=(cc == 0), stop=(cc == NCH - 1),
                                       skip_group_check=True)
                    return ins
                return go
            b.op("tensor", summ(c), waits=[s_gld], inc=s_t)
            b.op("scalar", (lambda cc: lambda s: s.activation(
                sqr[:, cc % 2, :], xb[:, cc % 2, :], AF.Square))(c),
                waits=[s_gld] + ([(s_t, s_t.n)] if c >= 2 else []), inc=s_s)
            def sqmm(cc):
                def go(t):
                    for ti, (o, w) in enumerate(EST):
                        ins = t.matmul(p_mm[0:1, 3 + ti, 0:w], onesb[:],
                                       sqr[:, cc % 2, o:o + w],
                                       start=(cc == 0), stop=(cc == NCH - 1),
                                       skip_group_check=True)
                    return ins
                return go
            b.op("tensor", sqmm(c), waits=[s_s], inc=s_t)
        STM = (s_t, s_t.n)
        def vstat(v):
            for ti, (o, w) in enumerate(EST):
                ins = v.tensor_scalar_mul(mrow[:, o:o + w], p_mm[0:1, ti, 0:w], 1.0 / C)
            return ins
        b.op("vector", vstat, waits=[STM], inc=s_v)
        b.op("vector", lambda v: v.tensor_tensor(vrow[:], mrow[:], mrow[:], ALU.mult),
             waits=[s_v], inc=s_v)
        def vstat2(v):
            for ti, (o, w) in enumerate(EST):
                ins = v.scalar_tensor_tensor(vrow[:, o:o + w], p_mm[0:1, 3 + ti, 0:w],
                                             1.0 / C, vrow[:, o:o + w],
                                             op0=ALU.mult, op1=ALU.subtract)
            return ins
        b.op("vector", vstat2, waits=[s_v], inc=s_v)
        b.op("scalar", lambda s: s.activation(vrow[:], vrow[:], AF.Sqrt, bias=epsb[:]),
             waits=[s_v, CONSTS], inc=s_s)
        b.op("vector", lambda v: v.reciprocal(rrow[:], vrow[:]), waits=[s_s], inc=s_v)
        RS = (s_v, s_v.n)
        def bcm(t):
            for ti, (o, w) in enumerate(EST):
                t.matmul(p_mm[:, ti, 0:w], ones1f[:], mrow[:, o:o + w],
                         start=True, stop=True)
            for ti, (o, w) in enumerate(EST):
                ins = t.matmul(p_mm[:, 3 + ti, 0:w], ones1f[:], rrow[:, o:o + w],
                               start=True, stop=True)
            return ins
        b.op("tensor", bcm, waits=[RS], inc=s_t)
        def bce(s):
            for ti, (o, w) in enumerate(EST):
                s.copy(mbc[:, o:o + w], p_mm[:, ti, 0:w])
            for ti, (o, w) in enumerate(EST):
                ins = s.copy(rbc[:, o:o + w], p_mm[:, 3 + ti, 0:w])
            return ins
        b.op("scalar", bce, waits=[s_t], inc=s_s)
        BCD = (s_s, s_s.n)
        wt2 = {}
        for f2 in range(4):
            b.op("gpsimd", (lambda s_, ff: lambda g: g.dma_start(
                out=wtile[:, ff % 4, :, :],
                in_=s_.rearrange("p (c f) -> p c f", c=NCH)))(wqT[f2], f2),
                waits=([] if f2 < 2 else [s_t, s_v, s_s]),
                inc=s_gld, dma=True)
            wt2[f2] = (s_gld, s_gld.n)
        xn2_marks = {}
        for c in range(NCH):
            b.op("sync", (lambda cc: lambda sy: sy.dma_start(
                out=xf32[:, cc % 2, :], in_=xTe[cc * 128:(cc + 1) * 128, :]))(c),
                waits=([(s_v, s_v.n - 3)] if c >= 2 else []), inc=s_yld, dma=True)
            b.op("vector", (lambda cc: lambda v: v.tensor_tensor(
                scrx[:], xf32[:, cc % 2, :], mbc[:], ALU.subtract))(c),
                waits=[BCD, s_yld], inc=s_v)
            b.op("vector", (lambda cc: lambda v: v.tensor_tensor(
                xstage[:, cc % 2, :], scrx[:], rbc[:], ALU.mult))(c),
                waits=[s_v] + ([(s_st, s_st.n - 16)] if c >= 2 else []), inc=s_v)
            b.op("vector", (lambda cc: lambda v: v.tensor_copy(
                xnb[:, cc, :], xstage[:, cc % 2, :]))(c),
                waits=[s_v], inc=s_v)
            xn2_marks[c] = s_v.n
            b.op("gpsimd", (lambda cc: lambda g: g.dma_start(
                out=d_xn[cc * 128:(cc + 1) * 128, :], in_=xnb[:, cc, :]))(c),
                waits=[s_v], inc=s_st, dma=True)
            b.op("gpsimd", (lambda cc: lambda g: g.dma_start(
                out=d_xnf[cc * 128:(cc + 1) * 128, :], in_=xstage[:, cc % 2, :]))(c),
                waits=[s_v], inc=s_st, dma=True)
        XN = (s_v, s_v.n)

        qev_marks = {}
        elu_marks = {}
        for f in range(NCH):
            if f >= 4:
                b.op("gpsimd", (lambda s_, ff: lambda g: g.dma_start(
                    out=wtile[:, ff % 4, :, :],
                    in_=s_.rearrange("p (c f) -> p c f", c=NCH)))(wqT[f], f),
                    waits=[(s_wf, s_wf.n - 3)], inc=s_gld, dma=True)
            def mmq(ff):
                def go(t):
                    for ti, (o, w) in enumerate(EST):
                        for c in range(NCH):
                            ins = t.matmul(p_mm[:, 3 * (ff % 2) + ti, 0:w],
                                           wtile[:, ff % 4, c, :], xnb[:, c, o:o + w],
                                           start=(c == 0), stop=(c == NCH - 1),
                                           skip_group_check=True)
                    return ins
                return go
            if f == 0:
                def mmq0(tl, cs_, first, last):
                    def go(t):
                        ins = None
                        for ti, (o, w) in enumerate(tl):
                            for c in cs_:
                                ins = t.matmul(p_mm[:, ti, 0:w],
                                               wtile[:, 0, c, :], xnb[:, c, o:o + w],
                                               start=(first and c == cs_[0]),
                                               stop=(last and c == cs_[-1]),
                                               skip_group_check=True)
                        return ins
                    return go
                for cg in range(4):
                    cs_ = list(range(cg * 4, cg * 4 + 4))
                    b.op("tensor", mmq0(list(EST), cs_, cg == 0, cg == 3),
                         waits=[wt2[0], (s_v, xn2_marks[cg * 4 + 3])],
                         inc=(s_t if cg == 3 else None),
                         inc2=(s_wf if cg == 3 else None))
            else:
                b.op("tensor", mmq(f),
                     waits=[(wt2[f] if f < 4 else s_gld), XN,
                            (s_s, qev_marks.get(f - 2, 0)),
                            (s_v, elu_marks.get(f - 2, 0))], inc=s_t, inc2=s_wf)
            MMF = (s_t, s_t.n)
            for ti, (o, w) in enumerate(EST):
                bank = 3 * (f % 2) + ti
                b.op("vector", (lambda bk, ww: lambda v: v.tensor_scalar_min(
                    scra[:, 0:ww], p_mm[:, bk, 0:ww], 0.0))(bank, w),
                    waits=[MMF, s_s, s_v], inc=s_v)
                b.op("vector", (lambda bk, ww: lambda v: v.tensor_scalar_max(
                    scrc[:, 0:ww], p_mm[:, bk, 0:ww], 0.0))(bank, w),
                    waits=[MMF], inc=s_v)
                b.op("scalar", (lambda ww: lambda s: s.activation(
                    scrb[:, 0:ww], scra[:, 0:ww], AF.Exp))(w),
                    waits=[s_v], inc=s_s)
                b.op("vector", (lambda ff, oo, ww: lambda v: v.scalar_tensor_tensor(
                    sqb[:, ff, oo:oo + ww], scrb[:, 0:ww], -1.0, scrc[:, 0:ww],
                    op0=ALU.add, op1=ALU.add))(f, o, w),
                    waits=[s_s, s_v], inc=s_v)
            def qev(ff):
                def go(s):
                    ins = None
                    for ti, (o, w) in enumerate(EST):
                        ins = s.copy(obuf[:, ff % 2, o:o + w],
                                     p_mm[:, 3 * (ff % 2) + ti, 0:w])
                    return ins
                return go
            b.op("scalar", qev(f), waits=[MMF, s_st], inc=s_s)
            qev_marks[f] = s_s.n
            elu_marks[f] = s_v.n
            b.op("scalar", (lambda ff: lambda s: s.dma_start(
                out=d_q[ff * 128:(ff + 1) * 128, :], in_=obuf[:, ff % 2, :]))(f),
                waits=[s_s], inc=s_st, dma=True)
        SQA = (s_v, s_v.n)

        for qg in range(NQ):
            for ti, (o, w) in enumerate(EST):
                def dmm(qq, oo, ww):
                    def go(t):
                        for ci in range(4):
                            ins = t.matmul(p_bc[0:1, 1, 0:ww], memnb[:, ci, :],
                                           sqb[:, qq * 4 + ci, oo:oo + ww],
                                           start=(ci == 0), stop=(ci == 3))
                        return ins
                    return go
                b.op("tensor", dmm(qg, o, w), waits=[SQA, s_v, s_s], inc=s_t)
                b.op("vector", (lambda ww: lambda v: v.reciprocal(
                    drow[:, 0:ww], p_bc[0:1, 1, 0:ww]))(w), waits=[s_t], inc=s_v)
                b.op("tensor", (lambda ww: lambda t: t.matmul(
                    p_bc[:, 0, 0:ww], ones1f[:], drow[:, 0:ww],
                    start=True, stop=True))(w), waits=[s_v, s_s], inc=s_t)
                b.op("scalar", (lambda ww: lambda s: s.copy(
                    dbc[:, 0:ww], p_bc[:, 0, 0:ww]))(w), waits=[s_t], inc=s_s)
                DBC = (s_s, s_s.n)
                for fo in range(4):
                    def amm(qq, oo, ww, ff):
                        def go(t):
                            for ci in range(4):
                                ins = t.matmul(p_mm[:, ff % 2, 0:ww],
                                               memb[:, ci, ff * 128:(ff + 1) * 128],
                                               sqb[:, qq * 4 + ci, oo:oo + ww],
                                               start=(ci == 0), stop=(ci == 3))
                            return ins
                        return go
                    b.op("tensor", amm(qg, o, w, fo), waits=[s_v], inc=s_t)
                    b.op("vector", (lambda ff, ww: lambda v: v.tensor_tensor(
                        amb[:, ff % 2, 0:ww], p_mm[:, ff % 2, 0:ww],
                        dbc[:, 0:ww], ALU.mult))(fo, w),
                        waits=[s_t, DBC, s_st], inc=s_v)
                    b.op("gpsimd", (lambda qq, oo, ww, ff: lambda g: g.dma_start(
                        out=d_am[qq * 512 + ff * 128:qq * 512 + (ff + 1) * 128, oo:oo + ww],
                        in_=amb[:, ff % 2, 0:ww]))(qg, o, w, fo),
                        waits=[s_v], inc=s_st, dma=True)
    with ExitStack() as _es2:
        _phase2(_es2)
    barrier()
    barrier()

    def _phase3(ph):
        F = ph.enter_context
        qb = F(sbt("p3_qb", [128, NCH, EP], BF16))
        kb = F(sbt("p3_kb", [128, 4, T], BF16))
        vab = F(sbt("p3_vab", [128, TKC, NH * 65], BF16))
        ptile = F(sbt("p3_pt", [128, 2, 2, 512], BF16))
        rr = F(sbt("p3_rr", [128, 2, 512], F32))
        albcr = F(sbt("p3_albcr", [128, 2, 512], F32))
        albc = F(sbt("p3_albc", [128, 2, 512], F32))
        stage = F(sbt("p3_stage", [128, 2, 2, 512], F32))
        p_qk = F(pst("p3_qk", [128, 4, 512], F32))
        p_pv = F(pst("p3_pv", [128, 4, 512], F32))

        b.op("gpsimd", lambda g: g.dma_start(out=qb[:, :, :], in_=fm(d_q)), inc=s_gld, dma=True)
        b.op("gpsimd", lambda g: g.dma_start(out=kb[:, :, :], in_=fm(d_k)), inc=s_gld, dma=True)
        b.op("gpsimd", lambda g: g.dma_start(
            out=vab[:, :, :], in_=d_va.rearrange("(tc p) f -> p tc f", p=128)),
            inc=s_gld, dma=True)
        LOADS = (s_gld, s_gld.n)
        conv_marks.append((s_gld, s_gld.n))
        for dst_, src_ in [(b_sf1, sf1T), (b_sf2, sf2T),
                           (b_fc1, fc1T), (b_fc2, fc2T), (b_pj, pjT), (b_sp, spT),
                           (b_wq1, wq1), (b_wk1, wk1), (b_wv1, wv1), (b_wo1, wo1),
                           (b_wg, wgT), (b_wu, wuT), (b_wd, wdT)]:
            b.op("gpsimd", (lambda d__, s__: lambda g: g.dma_start(
                out=d__, in_=s__))(dst_, src_), inc=s_gld, dma=True)
            conv_marks.append((s_gld, s_gld.n))

        bank_rel = {0: 0, 1: 0}
        qk_done = {}
        recip_mark = {0: 0}
        pv_guard = {0: 0}
        stage_marks = {}
        combo = 0
        for (qo, qw) in EST:
            for qg in range(NQ):
                for hp in range(NH // 2):
                    j = qg * 4 + hp
                    hA, hB = 2 * hp, 2 * hp + 1
                    cp = combo % 2
                    pv_guard[0] = stage_marks.get(combo - 2, 0)
                    def emit_qk(tt):
                        par = tt % 2
                        b.op("tensor", (lambda hh, tt_, jj, oo, ww, pp: lambda t: (
                            t.matmul(p_qk[:, 2 * pp, 0:ww],
                                     kb[0:64, hh, tt_ * 128:(tt_ + 1) * 128],
                                     qb[0:64, jj, oo:oo + ww], start=True, stop=True,
                                     tile_position=(0, 0)),
                            t.matmul(p_qk[:, 2 * pp + 1, 0:ww],
                                     kb[64:128, hh, tt_ * 128:(tt_ + 1) * 128],
                                     qb[64:128, jj, oo:oo + ww], start=True, stop=True,
                                     tile_position=(64, 0)))[-1])(hp, tt, j, qo, qw, par),
                            waits=[LOADS, (s_s, bank_rel[par]), (s_v, pv_guard[0])],
                            inc=s_t)
                        qk_done[tt] = s_t.n
                    emit_qk(0)
                    for tc in range(TKC):
                        par = tc % 2
                        b.op("scalar", (lambda tt, ww, pp: lambda s: s.activation(
                            ptile[:, tt % 2, :, 0:ww],
                            p_qk[:, 2 * pp:2 * pp + 2, 0:ww], AF.Exp, scale=0.125))(
                                tc, qw, par),
                            waits=[(s_t, qk_done[tc])], inc=s_s)
                        bank_rel[par] = s_s.n
                        EXPD = (s_s, s_s.n)
                        if tc + 1 < TKC:
                            emit_qk(tc + 1)
                        b.op("tensor", (lambda tt, ha, hb, cc, ww: lambda t: (
                            t.matmul(p_pv[0:65, 2 * cc, 0:ww],
                                     vab[:, tt, ha * 65:(ha + 1) * 65],
                                     ptile[:, tt % 2, 0, 0:ww],
                                     start=(tt == 0), stop=(tt == TKC - 1)),
                            t.matmul(p_pv[0:65, 2 * cc + 1, 0:ww],
                                     vab[:, tt, hb * 65:(hb + 1) * 65],
                                     ptile[:, tt % 2, 1, 0:ww],
                                     start=(tt == 0), stop=(tt == TKC - 1)))[-1])(
                                tc, hA, hB, cp, qw),
                            waits=[EXPD],
                            inc=(s_t if tc == TKC - 1 else None))
                    b.op("scalar", (lambda cc, ww: lambda s: (
                        s.copy(rr[64:65, 0, 0:ww], p_pv[64:65, 2 * cc, 0:ww]),
                        s.copy(rr[64:65, 1, 0:ww], p_pv[64:65, 2 * cc + 1, 0:ww]))[-1])(cp, qw),
                        waits=[(s_t, s_t.n)], inc=s_s)
                    b.op("tensor", (lambda ww: lambda t: (
                        t.matmul(p_qk[0:64, 2, 0:ww], onesf[64:65, 0:64], rr[64:65, 0, 0:ww],
                                 start=True, stop=True, tile_position=(64, 0)),
                        t.matmul(p_qk[0:64, 3, 0:ww], onesf[64:65, 0:64], rr[64:65, 1, 0:ww],
                                 start=True, stop=True, tile_position=(64, 0)))[-1])(qw),
                        waits=[(s_s, s_s.n), (s_s, bank_rel[1])], inc=s_t)
                    b.op("scalar", (lambda ww: lambda s: (
                        s.copy(albcr[0:64, 0, 0:ww], p_qk[0:64, 2, 0:ww]),
                        s.copy(albcr[0:64, 1, 0:ww], p_qk[0:64, 3, 0:ww]))[-1])(qw),
                        waits=[(s_t, s_t.n), (s_v, recip_mark[0])], inc=s_s)
                    bank_rel[1] = s_s.n
                    b.op("vector", (lambda ww: lambda v: v.reciprocal(
                        albc[0:64, :, 0:ww], albcr[0:64, :, 0:ww]))(qw),
                        waits=[(s_s, s_s.n)], inc=s_v)
                    recip_mark[0] = s_v.n
                    b.op("vector", (lambda cc, ww: lambda v: v.tensor_tensor(
                        stage[0:64, cc, 0, 0:ww], p_pv[0:64, 2 * cc, 0:ww],
                        albc[0:64, 0, 0:ww], ALU.mult))(cp, qw),
                        waits=[s_v, s_st], inc=s_v)
                    b.op("vector", (lambda cc, ww: lambda v: v.tensor_tensor(
                        stage[0:64, cc, 1, 0:ww], p_pv[0:64, 2 * cc + 1, 0:ww],
                        albc[0:64, 1, 0:ww], ALU.mult))(cp, qw),
                        waits=[s_v], inc=s_v)
                    stage_marks[combo] = s_v.n
                    b.op("sync", (lambda jj, oo, ww, cc: lambda sy: sy.dma_start(
                        out=d_ad[jj * 128:jj * 128 + 64, oo:oo + ww],
                        in_=stage[0:64, cc, 0, 0:ww]))(j, qo, qw, cp),
                        waits=[s_v], inc=s_st, dma=True)
                    b.op("sync", (lambda jj, oo, ww, cc: lambda sy: sy.dma_start(
                        out=d_ad[jj * 128 + 64:(jj + 1) * 128, oo:oo + ww],
                        in_=stage[0:64, cc, 1, 0:ww]))(j, qo, qw, cp),
                        waits=[s_v], inc=s_st, dma=True)
                    combo += 1
    with ExitStack() as _es3:
        _phase3(_es3)
    barrier()

    def _phase4(ph):
        F = ph.enter_context
        xnb = F(sbt("p4_xnb", [128, NCH, 512], BF16))
        hs = F(sbt("p4_hs", [128, NROW, 512], BF16))
        fcr = F(sbt("p4_fcr", [128, 2, NCH, 128], BF16))
        pjr = F(sbt("p4_pjr", [128, 2, 22, 128], BF16))
        gwb = F(sbt("p4_gwb", [128, NCH, NE], F32))
        xnfb = F(sbt("p4_xnfb", [128, NCH // 2, 128], F32))
        sgwb = F(sbt("p4_sgwb", [128, NCH, 1], BF16))
        selb = F(sbt("p4_selb", [9, 9 * 128], F32))
        comb_tm = F(sbt("p4_comb", [128, 4, 9], F32))
        pe_sb = F(sbt("p4_pe", [128, 8], F32))
        probs = F(sbt("p4_pr", [128, 8], F32))
        maskd = F(sbt("p4_mk", [128, 8], F32))
        rows1 = F(sbt("p4_r1", [128, 6], F32))
        combT = F(sbt("p4_cT", [9, 512], F32))
        cbc = F(sbt("p4_cbc", [128, 2, 512], F32))
        sg_t = F(sbt("p4_sgt", [128, 2, 512], F32))
        scr2 = F(sbt("p4_scr2", [128, 2, 512], F32))
        gstage = F(sbt("p4_gst", [128, 2, 512], F32))
        p_h = F(pst("p4_h", [128, 4, 512], F32))
        p_y = F(pst("p4_y", [128, 2, 512], F32))
        p_r = F(pst("p4_r", [128, 2, 512], F32))

        b.op("sync", lambda sy: sy.dma_start(out=gwb[:, :, :], in_=fm(gwT)), inc=s_yld, dma=True)
        b.op("gpsimd", lambda g: g.dma_start(out=sgwb[:, :, 0], in_=sgwT), inc=s_gld, dma=True)
        b.op("sync", lambda sy: sy.dma_start(out=selb[:], in_=c_sel), inc=s_yld, dma=True)
        P4L = ((s_gld, s_gld.n), (s_yld, s_yld.n))
        CONVD = conv_marks[-1]

        nfc = 0
        npj = 0
        YMARK = [0, 0]
        for ti, (o, w) in enumerate(EST):
            ntc = w // 128
            b.op("gpsimd", (lambda oo, ww: lambda g: g.dma_start(
                out=xnb[:, :, 0:ww], in_=fm(d_xn)[:, :, oo:oo + ww]))(o, w),
                waits=[(s_t, s_t.n), (s_v, s_v.n)], inc=s_gld, dma=True)
            XNL = (s_gld, s_gld.n)
            for tcc in range(ntc):
                tsl = slice(o + tcc * 128, o + (tcc + 1) * 128)
                for half in range(2):
                    b.op("sync", (lambda ts2, hh: lambda sy: sy.dma_start(
                        out=xnfb[:, :, :], in_=fm(d_xnf)[:, hh * 8:(hh + 1) * 8, ts2]))(tsl, half),
                        waits=[(s_t, s_t.n)], inc=s_yld, dma=True)
                    def rmm(ti_, hh):
                        def go(t):
                            for c in range(8):
                                ins = t.matmul(p_r[:, ti_ % 2, 0:8], xnfb[:, c, :],
                                               gwb[:, hh * 8 + c, :],
                                               start=(hh == 0 and c == 0),
                                               stop=(hh == 1 and c == 7))
                            return ins
                        return go
                    b.op("tensor", rmm(tcc, half),
                         waits=[P4L[1], s_yld] + ([s_s, s_v] if half == 0 else []),
                         inc=s_t)
                RMM = (s_t, s_t.n)
                b.op("vector", (lambda tt: lambda v: v.reduce_max(
                    rows1[:, 0:1], p_r[:, tt % 2, 0:8], axis=AX.X, negate=True))(tcc),
                    waits=[RMM], inc=s_v)
                b.op("scalar", (lambda tt: lambda s: s.activation(
                    pe_sb[:], p_r[:, tt % 2, 0:8], AF.Exp, bias=rows1[:, 0:1]))(tcc),
                    waits=[RMM, s_v], inc=s_s)
                b.op("vector", lambda v: v.reduce_sum(rows1[:, 1:2], pe_sb[:], axis=AX.X),
                     waits=[s_s], inc=s_v)
                b.op("vector", lambda v: v.reciprocal(rows1[:, 2:3], rows1[:, 1:2]),
                     waits=[s_v], inc=s_v)
                b.op("vector", lambda v: v.tensor_scalar_mul(probs[:], pe_sb[:], rows1[:, 2:3]),
                     waits=[s_v], inc=s_v)
                b.op("vector", lambda v: v.reduce_max(rows1[:, 3:4], probs[:], axis=AX.X),
                     waits=[s_v], inc=s_v)
                b.op("vector", lambda v: v.tensor_scalar(maskd[:], probs[:], rows1[:, 3:4],
                                                         None, op0=ALU.is_ge),
                     waits=[s_v], inc=s_v)
                b.op("vector", lambda v: v.scalar_tensor_tensor(
                    maskd[:], maskd[:], -2.0, probs[:], op0=ALU.mult, op1=ALU.add),
                    waits=[s_v], inc=s_v)
                b.op("vector", lambda v: v.reduce_max(rows1[:, 4:5], maskd[:], axis=AX.X),
                     waits=[s_v], inc=s_v)
                b.op("vector", lambda v: v.tensor_scalar(maskd[:], probs[:], rows1[:, 4:5],
                                                         None, op0=ALU.is_ge),
                     waits=[s_v], inc=s_v)
                b.op("vector", (lambda tt: lambda v: v.tensor_tensor(
                    comb_tm[:, tt, 0:8], probs[:], maskd[:], ALU.mult))(tcc),
                    waits=[s_v], inc=s_v)
                b.op("tensor", (lambda tt, ls: lambda t: [
                    t.matmul(p_r[:, tt % 2, 8:9], xnb[:, c, ls],
                             sgwb[:, c, :], start=(c == 0), stop=(c == NCH - 1))
                    for c in range(NCH)][-1])(tcc, slice(tcc * 128, (tcc + 1) * 128)),
                    waits=[XNL, s_s, s_v], inc=s_t)
                b.op("scalar", (lambda tt: lambda s: s.activation(
                    comb_tm[:, tt, 8:9], p_r[:, tt % 2, 8:9], AF.Sigmoid))(tcc),
                    waits=[s_t], inc=s_s)
            CHAINS = (s_s, s_s.n)
            for tt in range(ntc):
                b.op("tensor", (lambda ti_: lambda t: t.transpose(
                    p_r[0:9, ti_ % 2, 0:128], comb_tm[:, ti_, :], eyeb[:]))(tt),
                    waits=[CHAINS, (s_v, s_v.n), CONSTS, s_s], inc=s_t)
                b.op("scalar", (lambda ti_: lambda s: s.copy(
                    combT[:, ti_ * 128:(ti_ + 1) * 128], p_r[0:9, ti_ % 2, 0:128]))(tt),
                    waits=[s_t], inc=s_s)
            COMBT = (s_s, s_s.n)

            hidx = 0
            silu_marks = {}
            hs_marks = {}
            YMARK[0] = YMARK[1]
            for grp in range(1 + NE):
                shared = (grp == 0)
                nf = NFS if shared else NFM
                if not shared:
                    e = grp - 1
                    b.op("tensor", (lambda ee, ww: lambda t: t.matmul(
                        p_r[:, ee % 2, 0:ww], selb[:, ee * 128:(ee + 1) * 128],
                        combT[:, 0:ww], start=True, stop=True))(e, w),
                        waits=[COMBT, s_s], inc=s_t)
                    b.op("scalar", (lambda ee, ww: lambda s: s.copy(
                        cbc[:, ee % 2, 0:ww], p_r[:, ee % 2, 0:ww]))(e, w),
                        waits=[s_t, s_v], inc=s_s)
                else:
                    b.op("tensor", (lambda ww: lambda t: t.matmul(
                        p_r[:, 0, 0:ww], selb[:, 8 * 128:9 * 128], combT[:, 0:ww],
                        start=True, stop=True))(w),
                        waits=[COMBT, s_s], inc=s_t)
                    b.op("scalar", (lambda ww: lambda s: s.copy(
                        cbc[:, 0, 0:ww], p_r[:, 0, 0:ww]))(w),
                        waits=[s_t, s_v], inc=s_s)
                CBC = (s_s, s_s.n)
                for f in range(nf):
                    for which in range(2):
                        if shared:
                            src_ = (b_sf1 if which == 0 else b_sf2)[f]
                        else:
                            src_ = (b_fc1 if which == 0 else b_fc2)[e, f]
                        qname = "sync" if which == 0 else "gpsimd"
                        qsem = s_yld if which == 0 else s_gld
                        b.op(qname, (lambda s_, l_: lambda g: g.dma_start(
                            out=fcr[:, l_ % 2, :, :],
                            in_=s_.rearrange("p (c f) -> p c f", c=NCH)))(src_, nfc),
                            waits=([(s_wf, s_wf.n - 1)] if nfc >= 2 else []) +
                                  ([CONVD] if nfc < 2 else []),
                            inc=qsem, dma=True)
                        def hmm(l_, wh, fp, ww):
                            def go(t):
                                for c in range(NCH):
                                    ins = t.matmul(p_h[:, 2 * fp + wh, 0:ww],
                                                   fcr[:, l_ % 2, c, :], xnb[:, c, 0:ww],
                                                   start=(c == 0), stop=(c == NCH - 1))
                                return ins
                            return go
                        hw_ = ([(s_s, silu_marks.get(hidx - 2, 0)),
                                (s_v, hs_marks.get(hidx - 2, 0))]
                               if which == 0 else
                               [(s_v, hs_marks.get(hidx - 2, 0))])
                        b.op("tensor", hmm(nfc, which, hidx % 2, w),
                             waits=[qsem, XNL] + hw_,
                             inc=(s_t if which == 1 else None), inc2=s_wf)
                        nfc += 1
                    HMM = (s_t, s_t.n)
                    b.op("scalar", (lambda fp, ww: lambda s: s.activation(
                        sg_t[:, fp, 0:ww], p_h[:, 2 * fp, 0:ww], AF.Silu))(hidx % 2, w),
                        waits=[HMM], inc=s_s)
                    silu_marks[hidx] = s_s.n
                    b.op("vector", (lambda fp, ww: lambda v: v.tensor_tensor(
                        scr2[:, fp, 0:ww], p_h[:, 2 * fp + 1, 0:ww], sg_t[:, fp, 0:ww],
                        ALU.mult))(hidx % 2, w),
                        waits=[(s_s, s_s.n)], inc=s_v)
                    row = f if shared else NFS + (grp - 1) * NFM + f
                    b.op("vector", (lambda rr_, cc, ww, fp: lambda v: v.tensor_tensor(
                        hs[:, rr_, 0:ww], scr2[:, fp, 0:ww], cbc[:, cc, 0:ww], ALU.mult))(
                            row, 0 if shared else (grp - 1) % 2, w, hidx % 2),
                        waits=[(s_v, s_v.n), CBC, (s_t, YMARK[0])], inc=s_v)
                    hs_marks[hidx] = s_v.n
                    hidx += 1
            HS_ALL = (s_v, s_v.n)

            gsilu_marks = {}
            for c in range(NCH):
                subs = [("sh", 0, 22), ("sh", 22, 22)] + [("e", e_, NFM) for e_ in range(NE)]
                for si, (kind, idx, nrows) in enumerate(subs):
                    if kind == "sh":
                        src_ = b_sp[c, :, idx * 128:(idx + 22) * 128]
                        hrow = idx
                    else:
                        src_ = b_pj[idx, c]
                        hrow = NFS + idx * NFM
                    qname = "sync" if npj % 2 == 0 else "gpsimd"
                    qsem2 = s_yld if npj % 2 == 0 else s_gld
                    b.op(qname, (lambda s_, l_, nr: lambda g: g.dma_start(
                        out=pjr[:, l_ % 2, 0:nr, :],
                        in_=s_.rearrange("p (f c) -> p f c", f=nr)))(src_, npj, nrows),
                        waits=([(s_pf, s_pf.n - 1)] if npj >= 2 else []), inc=qsem2, dma=True)
                    def ymm(l_, nr, hr, cc, first, last, ww):
                        def go(t):
                            for f in range(nr):
                                ins = t.matmul(p_y[:, cc % 2, 0:ww], pjr[:, l_ % 2, f, :],
                                               hs[:, hr + f, 0:ww],
                                               start=(first and f == 0),
                                               stop=(last and f == nr - 1))
                            return ins
                        return go
                    b.op("tensor", ymm(npj, nrows, hrow, c, si == 0, si == len(subs) - 1, w),
                         waits=[qsem2, HS_ALL] +
                               ([(s_s, gsilu_marks.get(c - 2, 0))] if si == 0 else []),
                         inc=(s_t if si == len(subs) - 1 else None), inc2=s_pf)
                    npj += 1
                YMM = (s_t, s_t.n)
                YMARK[1] = s_t.n
                b.op("scalar", (lambda cc, ww: lambda s: s.activation(
                    gstage[:, cc % 2, 0:ww], p_y[:, cc % 2, 0:ww], AF.Silu))(c, w),
                    waits=[YMM, s_st], inc=s_s)
                gsilu_marks[c] = s_s.n
                b.op("scalar", (lambda cc, oo, ww: lambda s: s.dma_start(
                    out=d_g[cc * 128:(cc + 1) * 128, oo:oo + ww],
                    in_=gstage[:, cc % 2, 0:ww]))(c, o, w),
                    waits=[s_s], inc=s_st, dma=True)
    with ExitStack() as _es4:
        _phase4(_es4)
    barrier()

    def _phase5(ph):
        F = ph.enter_context
        xn1b = F(sbt("p5_xn1", [128, NCH, EP], BF16))
        x1t = F(sbt("p5_x1t", [128, 2, EP], F32))
        x1h = F(sbt("p5_x1h", [128, 2, EP], BF16))
        ld3 = F(sbt("p5_ld3", [128, 3, EP], F32))
        amt = F(sbt("p5_amt", [128, 1, EP], BF16))
        sqr = F(sbt("p5_sqr", [128, 2, EP], BF16))
        wtile = F(sbt("p5_wt", [128, 2, NCH, 128], BF16))
        wvtile = F(sbt("p5_wv", [128, 2, NCH, 512], BF16))
        ctabb = F(sbt("p5_ct", [128, EP], F32))
        stabb = F(sbt("p5_st", [128, EP], F32))
        pib = F(sbt("p5_pi", [128, 128], BF16))
        w128 = F(sbt("p5_w128", [128, 1], F32))
        omw128 = F(sbt("p5_omw", [128, 1], F32))
        mrow = F(sbt("p5_mrow", [1, EP], F32))
        vrow = F(sbt("p5_vrow", [1, EP], F32))
        rrow = F(sbt("p5_rrow", [1, EP], F32))
        mbc = F(sbt("p5_mbc", [128, EP], F32))
        rbc = F(sbt("p5_rbc", [128, EP], F32))
        scr1 = F(sbt("p5_scr1", [128, EP], F32))
        scr2 = F(sbt("p5_scr2", [128, 2, 512], F32))
        scr3 = F(sbt("p5_scr3", [128, 2, 512], F32))
        qbf = F(sbt("p5_qbf", [128, 2, EP], BF16))
        obuf = F(sbt("p5_obuf", [128, 2, EP], BF16))
        vstage = F(sbt("p5_vst", [128, 2, 512], BF16))
        p_mm = F(pst("p5_mm", [128, 6, 512], F32))
        p_rt = F(pst("p5_rt", [128, 2, 512], F32))

        b.op("gpsimd", lambda g: g.dma_start(out=pib[:], in_=pim), inc=s_gld, dma=True)
        b.op("sync", lambda sy: sy.dma_start(out=ctabb[:], in_=ctab), inc=s_yld, dma=True)
        b.op("sync", lambda sy: sy.dma_start(out=stabb[:], in_=stab), inc=s_yld, dma=True)
        b.op("sync", lambda sy: sy.dma_start(out=w128[0:1, :], in_=mw), inc=s_yld, dma=True)
        P5L = (s_yld, s_yld.n)
        b.op("scalar", lambda s: s.activation(w128[0:1, :], w128[0:1, :], AF.Sigmoid),
             waits=[P5L], inc=s_s)
        b.op("tensor", lambda t: t.matmul(p_rt[:, 0, 0:1], ones1f[:], w128[0:1, :],
                                          start=True, stop=True), waits=[s_s], inc=s_t)
        b.op("scalar", lambda s: s.copy(w128[:], p_rt[:, 0, 0:1]), waits=[s_t], inc=s_s)
        b.op("vector", lambda v: v.tensor_scalar(omw128[:], w128[:], -1.0, 1.0,
                                                 op0=ALU.mult, op1=ALU.add),
             waits=[s_s], inc=s_v)
        WMIX = ((s_v, s_v.n), (s_s, s_s.n))

        EH = EP // 2
        x1m = {}
        t5m = {}
        s5m = {}
        for c in range(NCH):
            for hh in range(2):
                hsl = slice(hh * EH, (hh + 1) * EH)
                wld = [(s_v, x1m[(c - 1, hh)])] if c >= 1 else []
                b.op("sync", (lambda cc, ss: lambda sy: sy.dma_start(
                    out=ld3[:, 0, ss], in_=xTe[cc * 128:(cc + 1) * 128, ss]))(c, hsl),
                    waits=wld, inc=s_yld, dma=True)
                b.op("sync", (lambda cc, ss: lambda sy: sy.dma_start(
                    out=ld3[:, 1, ss], in_=d_ad[cc * 128:(cc + 1) * 128, ss]))(c, hsl),
                    inc=s_yld, dma=True)
                b.op("sync", (lambda cc, ss: lambda sy: sy.dma_start(
                    out=ld3[:, 2, ss], in_=d_g[cc * 128:(cc + 1) * 128, ss]))(c, hsl),
                    inc=s_yld, dma=True)
                b.op("gpsimd", (lambda cc, ss: lambda g: g.dma_start(
                    out=amt[:, 0, ss], in_=fm(d_am)[:, cc, ss]))(c, hsl),
                    waits=wld, inc=s_gld, dma=True)
                b.op("vector", (lambda ss: lambda v: v.tensor_scalar_mul(
                    scr1[:, ss], amt[:, 0, ss], w128[:]))(hsl),
                    waits=[WMIX[0], s_yld, s_gld], inc=s_v)
                b.op("vector", (lambda ss: lambda v: v.scalar_tensor_tensor(
                    scr1[:, ss], ld3[:, 1, ss], omw128[:], scr1[:, ss],
                    op0=ALU.mult, op1=ALU.add))(hsl),
                    waits=[s_v], inc=s_v)
                b.op("vector", (lambda ss: lambda v: v.tensor_tensor(
                    scr1[:, ss], scr1[:, ss], ld3[:, 2, ss], ALU.mult))(hsl),
                    waits=[s_v], inc=s_v)
                b.op("vector", (lambda cc, ss: lambda v: v.tensor_tensor(
                    x1t[:, cc % 2, ss], scr1[:, ss], ld3[:, 0, ss], ALU.add))(c, hsl),
                    waits=[s_v, (s_t, t5m.get(c - 2, 0)), (s_s, s5m.get(c - 2, 0))]
                          + ([(s_st, s_st.n - 16)] if c >= 2 else []),
                    inc=s_v)
                x1m[(c, hh)] = s_v.n
                b.op("vector", (lambda cc, ss: lambda v: v.tensor_copy(
                    x1h[:, cc % 2, ss], x1t[:, cc % 2, ss]))(c, hsl),
                    waits=[s_v], inc=s_v)
            b.op("gpsimd", (lambda cc: lambda g: g.dma_start(
                out=d_x1[cc * 128:(cc + 1) * 128, :], in_=x1t[:, cc % 2, :]))(c),
                waits=[s_v], inc=s_st, dma=True)
            def summ(cc):
                def go(t):
                    for ti, (o, w) in enumerate(EST):
                        ins = t.matmul(p_mm[0:1, ti, 0:w], onesb[:], x1h[:, cc % 2, o:o + w],
                                       start=(cc == 0), stop=(cc == NCH - 1),
                                       skip_group_check=True)
                    return ins
                return go
            b.op("tensor", summ(c), waits=[s_v], inc=s_t)
            b.op("scalar", (lambda cc: lambda s: s.activation(
                sqr[:, cc % 2, :], x1h[:, cc % 2, :], AF.Square))(c),
                waits=[s_v] + ([(s_t, s_t.n)] if c >= 2 else []), inc=s_s)
            def sqmm(cc):
                def go(t):
                    for ti, (o, w) in enumerate(EST):
                        ins = t.matmul(p_mm[0:1, 3 + ti, 0:w], onesb[:],
                                       sqr[:, cc % 2, o:o + w],
                                       start=(cc == 0), stop=(cc == NCH - 1),
                                       skip_group_check=True)
                    return ins
                return go
            b.op("tensor", sqmm(c), waits=[s_s], inc=s_t)
            t5m[c] = s_t.n
            s5m[c] = s_s.n
        STM = (s_t, s_t.n)
        def vstat(v):
            for ti, (o, w) in enumerate(EST):
                ins = v.tensor_scalar_mul(mrow[:, o:o + w], p_mm[0:1, ti, 0:w], 1.0 / C)
            return ins
        b.op("vector", vstat, waits=[STM], inc=s_v)
        b.op("vector", lambda v: v.tensor_tensor(vrow[:], mrow[:], mrow[:], ALU.mult),
             waits=[s_v], inc=s_v)
        def vstat2(v):
            for ti, (o, w) in enumerate(EST):
                ins = v.scalar_tensor_tensor(vrow[:, o:o + w], p_mm[0:1, 3 + ti, 0:w],
                                             1.0 / C, vrow[:, o:o + w],
                                             op0=ALU.mult, op1=ALU.subtract)
            return ins
        b.op("vector", vstat2, waits=[s_v], inc=s_v)
        b.op("scalar", lambda s: s.activation(vrow[:], vrow[:], AF.Sqrt, bias=epsb[:]),
             waits=[s_v, CONSTS], inc=s_s)
        b.op("vector", lambda v: v.reciprocal(rrow[:], vrow[:]), waits=[s_s], inc=s_v)
        RS = (s_v, s_v.n)
        def bcm(t):
            for ti, (o, w) in enumerate(EST):
                t.matmul(p_mm[:, ti, 0:w], ones1f[:], mrow[:, o:o + w], start=True, stop=True)
            for ti, (o, w) in enumerate(EST):
                ins = t.matmul(p_mm[:, 3 + ti, 0:w], ones1f[:], rrow[:, o:o + w],
                               start=True, stop=True)
            return ins
        b.op("tensor", bcm, waits=[RS], inc=s_t)
        def bce(s):
            for ti, (o, w) in enumerate(EST):
                s.copy(mbc[:, o:o + w], p_mm[:, ti, 0:w])
            for ti, (o, w) in enumerate(EST):
                ins = s.copy(rbc[:, o:o + w], p_mm[:, 3 + ti, 0:w])
            return ins
        b.op("scalar", bce, waits=[s_t], inc=s_s)
        BCD = (s_s, s_s.n)
        xn5 = {}
        for c in range(NCH):
            b.op("sync", (lambda cc: lambda sy: sy.dma_start(
                out=ld3[:, 0, :], in_=d_x1[cc * 128:(cc + 1) * 128, :]))(c),
                waits=[(s_v, s_v.n)], inc=s_yld, dma=True)
            b.op("vector", (lambda cc: lambda v: v.tensor_tensor(
                scr1[:], ld3[:, 0, :], mbc[:], ALU.subtract))(c),
                waits=[BCD, s_yld], inc=s_v)
            b.op("vector", (lambda cc: lambda v: v.tensor_tensor(
                xn1b[:, cc, :], scr1[:], rbc[:], ALU.mult))(c),
                waits=[s_v], inc=s_v)
            xn5[c] = s_v.n
            b.op("gpsimd", (lambda cc: lambda g: g.dma_start(
                out=d_xn1[cc * 128:(cc + 1) * 128, :], in_=xn1b[:, cc, :]))(c),
                waits=[s_v], inc=s_st, dma=True)
        XN1 = (s_v, s_v.n)

        p5_marks_s = {}
        p5_marks_v = {}
        wt01 = {}
        for which, (wsrc, dst, qoff, qlen) in enumerate(
                [(b_wq1, d_q1, W1, P), (b_wk1, d_k1, 0, EP)]):
            for f in range(NCH):
                widx = which * NCH + f
                if widx < 2:
                    b.op("scalar", (lambda s_, l_: lambda s: s.dma_start(
                        out=wtile[:, l_ % 2, :, :],
                        in_=s_.rearrange("p (c f) -> p c f", c=NCH)))(wsrc[f], widx),
                        inc=s_gld, dma=True)
                    wt01[widx] = (s_gld, s_gld.n)
                else:
                    b.op("gpsimd", (lambda s_, l_: lambda g: g.dma_start(
                        out=wtile[:, l_ % 2, :, :],
                        in_=s_.rearrange("p (c f) -> p c f", c=NCH)))(wsrc[f], widx),
                        waits=[(s_wf, s_wf.n - 1)],
                        inc=s_gld, dma=True)
                tiles = [(qoff + i * 512, min(512, qlen - i * 512))
                         for i in range((qlen + 511) // 512)]
                def mmp(l_, fp, tl):
                    def go(t):
                        for ti, (o, w) in enumerate(tl):
                            for c in range(NCH):
                                ins = t.matmul(p_mm[:, 3 * (fp % 2) + ti, 0:w],
                                               wtile[:, l_ % 2, c, :], xn1b[:, c, o:o + w],
                                               start=(c == 0), stop=(c == NCH - 1),
                                               skip_group_check=True)
                        return ins
                    return go
                if widx == 0:
                    def mmp0(tl, cs_, first, last):
                        def go(t):
                            ins = None
                            for ti, (o, w) in enumerate(tl):
                                for c in cs_:
                                    ins = t.matmul(p_mm[:, ti, 0:w],
                                                   wtile[:, 0, c, :], xn1b[:, c, o:o + w],
                                                   start=(first and c == cs_[0]),
                                                   stop=(last and c == cs_[-1]),
                                                   skip_group_check=True)
                            return ins
                        return go
                    for cg in range(4):
                        cs_ = list(range(cg * 4, cg * 4 + 4))
                        b.op("tensor", mmp0(tiles, cs_, cg == 0, cg == 3),
                             waits=[wt01[0], (s_v, xn5[cg * 4 + 3])],
                             inc=(s_t if cg == 3 else None),
                             inc2=(s_wf if cg == 3 else None))
                else:
                    b.op("tensor", mmp(widx, f, tiles),
                         waits=[(wt01[1] if widx == 1 else s_gld), XN1,
                                (s_s, p5_marks_s.get(widx - 2, 0)),
                                (s_v, p5_marks_v.get(widx - 2, 0))], inc=s_t, inc2=s_wf)
                MMF = (s_t, s_t.n)
                def qbc(fp, tl):
                    def go(s):
                        ins = None
                        for ti, (o, w) in enumerate(tl):
                            ins = s.copy(qbf[:, fp % 2, o:o + w],
                                         p_mm[:, 3 * (fp % 2) + ti, 0:w])
                        return ins
                    return go
                b.op("scalar", qbc(f, tiles), waits=[MMF], inc=s_s)
                for ti, (o, w) in enumerate(tiles):
                    b.op("tensor", (lambda fp, ti_, oo, ww: lambda t: t.matmul(
                        p_rt[:, ti_ % 2, 0:ww], pib[:], qbf[:, fp % 2, oo:oo + ww],
                        start=True, stop=True))(f, ti, o, w),
                        waits=[s_s, s_v], inc=s_t)
                    ROT = (s_t, s_t.n)
                    b.op("vector", (lambda fp, ti_, oo, ww: lambda v: v.tensor_tensor(
                        scr2[:, ti_ % 2, 0:ww], p_mm[:, 3 * (fp % 2) + ti_, 0:ww],
                        ctabb[:, oo:oo + ww], ALU.mult))(f, ti, o, w),
                        waits=[MMF, P5L, s_s], inc=s_v)
                    b.op("vector", (lambda ti_, oo, ww: lambda v: v.tensor_tensor(
                        scr3[:, ti_ % 2, 0:ww], p_rt[:, ti_ % 2, 0:ww],
                        stabb[:, oo:oo + ww], ALU.mult))(ti, o, w),
                        waits=[ROT], inc=s_v)
                    b.op("vector", (lambda fp, ti_, oo, ww, qo_: lambda v: v.tensor_tensor(
                        obuf[:, fp % 2, oo - qo_:oo - qo_ + ww], scr2[:, ti_ % 2, 0:ww],
                        scr3[:, ti_ % 2, 0:ww], ALU.add))(f, ti, o, w, qoff),
                        waits=[s_v, s_st], inc=s_v)
                p5_marks_s[widx] = s_s.n
                p5_marks_v[widx] = s_v.n
                b.op("gpsimd", (lambda d_, fp, ql: lambda g: g.dma_start(
                    out=d_[fp * 128:(fp + 1) * 128, :], in_=obuf[:, fp % 2, 0:ql]))(
                        dst, f, qlen),
                    waits=[s_v], inc=s_st, dma=True)

        for dvt in range(4):
            b.op("gpsimd", (lambda dd: lambda g: g.dma_start(
                out=wvtile[:, dd % 2, :, :],
                in_=b_wv1[dd].rearrange("p (c f) -> p c f", c=NCH)))(dvt),
                waits=([(s_t, s_t.n)] if dvt >= 2 else []), inc=s_gld, dma=True)
            for tc in range(EP // 128):
                def vmm(dd, tt):
                    def go(t):
                        for c in range(NCH):
                            ins = t.matmul(p_rt[:, tt % 2, :],
                                           xn1b[:, c, tt * 128:(tt + 1) * 128],
                                           wvtile[:, dd % 2, c, :],
                                           start=(c == 0), stop=(c == NCH - 1))
                        return ins
                    return go
                b.op("tensor", vmm(dvt, tc), waits=[s_gld, XN1, s_s, s_v], inc=s_t)
                b.op("scalar", (lambda tt: lambda s: s.copy(
                    vstage[:, tt % 2, :], p_rt[:, tt % 2, :]))(tc),
                    waits=[s_t, s_st], inc=s_s)
                b.op("scalar", (lambda dd, tt: lambda s: s.dma_start(
                    out=d_v1[tt * 128:(tt + 1) * 128, dd * 512:(dd + 1) * 512],
                    in_=vstage[:, tt % 2, :]))(dvt, tc),
                    waits=[s_s], inc=s_st, dma=True)
    with ExitStack() as _es5:
        _phase5(_es5)
    barrier()

    def _phase6(ph):
        F = ph.enter_context
        qb = F(sbt("p6_qb", [128, NCH, P], BF16))
        kb = F(sbt("p6_kb", [128, NCH, EP], BF16))
        vb = F(sbt("p6_vb", [128, EP // 128, C], BF16))
        a_fm = F(sbt("p6_afm", [128, NCH, P], BF16))
        mskb = F(sbt("p6_msk", [128, 2, 4, 2, 256], BF16))
        ptile = F(sbt("p6_pt", [128, 4, 2, 256], BF16))
        praw = F(sbt("p6_praw", [128, 4, 2, 256], BF16))
        rrow = F(sbt("p6_rrow", [1, 512], F32))
        albc = F(sbt("p6_albc", [128, 512], F32))
        wtile = F(sbt("p6_wt", [128, 2, NCH, 128], BF16))
        obuf = F(sbt("p6_obuf", [128, 2, P], F32))
        p_qk = F(pst("p6_qk", [128, 4, 512], F32))
        p_pv = F(pst("p6_pv", [128, 2, 512], F32))
        p_al = F(pst("p6_al", [128, 2, 512], F32))

        b.op("gpsimd", lambda g: g.dma_start(out=qb[:, :, :], in_=fm(d_q1)), inc=s_gld, dma=True)
        b.op("gpsimd", lambda g: g.dma_start(out=kb[:, :, :], in_=fm(d_k1)), inc=s_gld, dma=True)
        b.op("gpsimd", lambda g: g.dma_start(
            out=vb[:, :, :], in_=d_v1.rearrange("(tc p) f -> p tc f", p=128)), inc=s_gld, dma=True)
        b.op("gpsimd", lambda g: g.dma_start(
            out=mskb[:, 0, :, :, :].rearrange("p a b c -> p (a b c)"), in_=m0), inc=s_gld, dma=True)
        b.op("gpsimd", lambda g: g.dma_start(
            out=mskb[:, 1, :, :, :].rearrange("p a b c -> p (a b c)"), in_=mb), inc=s_gld, dma=True)
        P6L = (s_gld, s_gld.n)

        SC1 = float(DH1) ** -0.5
        combo = 0
        for wc in range(4):
            mi = 0 if wc == 0 else 1
            for hp in range(H1 // 2):
                hA, hB = 2 * hp, 2 * hp + 1
                cp = combo % 2
                ssl = slice(wc * 256, (wc + 1) * 256)
                b.op("tensor", (lambda ha, hb, ww: lambda t: [
                    t.matmul(p_qk[:, kc, hi * 256:(hi + 1) * 256],
                             kb[:, hh, ww * 256 + kc * 128:ww * 256 + (kc + 1) * 128],
                             qb[:, hh, ww * 256:(ww + 1) * 256],
                             start=(hi == 0), stop=(hi == 1), skip_group_check=True)
                    for kc in range(4) for hi, hh in enumerate((ha, hb))][-1])(hA, hB, wc),
                    waits=[P6L, s_s, s_v], inc=s_t)
                QKM = (s_t, s_t.n)
                b.op("scalar", lambda s: s.activation(
                    praw[:, :, :, :].rearrange("p a b c -> p a (b c)"),
                    p_qk[:, :, :], AF.Exp, scale=SC1),
                    waits=[QKM], inc=s_s)
                b.op("vector", (lambda mm_: lambda v: v.tensor_tensor(
                    ptile[:, :, :, :], praw[:, :, :, :], mskb[:, mm_, :, :, :],
                    ALU.mult))(mi), waits=[s_s, s_t], inc=s_v)
                PT = (s_v, s_v.n)
                b.op("tensor", lambda t: [
                    t.matmul(p_al[0:1, 0, :], onesb[:], ptile[:, kc, :, :],
                             start=(kc == 0), stop=(kc == 3))
                    for kc in range(4)][-1],
                    waits=[PT, s_v], inc=s_t)
                b.op("tensor", (lambda ha, hb, ww, cc: lambda t: [
                    t.matmul(p_pv[:, cc, hi * 256:(hi + 1) * 256],
                             vb[:, ww * 2 + kc, hh * 128:(hh + 1) * 128],
                             ptile[:, kc, hi, :],
                             start=(hi == 0 and kc == 0), stop=(hi == 1 and kc == 3),
                             skip_group_check=True)
                    for hi, hh in enumerate((ha, hb)) for kc in range(4)][-1])(
                        hA, hB, wc, cp),
                    waits=[s_v], inc=s_t)
                PVM = (s_t, s_t.n)
                b.op("vector", lambda v: v.reciprocal(rrow[:], p_al[0:1, 0, :]),
                     waits=[(s_t, s_t.n - 1)], inc=s_v)
                b.op("tensor", lambda t: t.matmul(p_al[:, 1, :], ones1f[0:1, :], rrow[:],
                                                  start=True, stop=True),
                     waits=[s_v, s_s], inc=s_t)
                b.op("scalar", lambda s: s.copy(albc[:], p_al[:, 1, :]), waits=[s_t], inc=s_s)
                b.op("vector", (lambda ha, cc, qsl: lambda v: v.tensor_tensor(
                    a_fm[:, ha:ha + 2, qsl],
                    p_pv[:, cc, :].rearrange("p (h s) -> p h s", h=2),
                    albc[:, :].rearrange("p (h s) -> p h s", h=2), ALU.mult))(hA, cp, ssl),
                    waits=[PVM, s_s], inc=s_v)
                combo += 1
        AFM = (s_v, s_v.n)

        ev_marks = {}
        for f in range(NCH):
            b.op("gpsimd", (lambda s_, ff: lambda g: g.dma_start(
                out=wtile[:, ff % 2, :, :],
                in_=s_.rearrange("p (c f) -> p c f", c=NCH)))(b_wo1[f], f),
                waits=([(s_wf, s_wf.n - 1)] if f >= 2 else []), inc=s_gld, dma=True)
            def mmo(ff):
                def go(t):
                    for st in range(2):
                        for c in range(NCH):
                            ins = t.matmul(p_qk[:, 2 * (ff % 2) + st, :],
                                           wtile[:, ff % 2, c, :], a_fm[:, c, ST[st]],
                                           start=(c == 0), stop=(c == NCH - 1))
                    return ins
                return go
            b.op("tensor", mmo(f),
                 waits=[s_gld, AFM, (s_s, ev_marks.get(f - 2, 0))] if f >= 1 else
                       [s_gld, AFM, s_s],
                 inc=s_t, inc2=s_wf)
            def ev(ff):
                def go(s):
                    s.copy(obuf[:, ff % 2, 0:512], p_qk[:, 2 * (ff % 2), :])
                    return s.copy(obuf[:, ff % 2, 512:1024], p_qk[:, 2 * (ff % 2) + 1, :])
                return go
            b.op("scalar", ev(f), waits=[s_t, s_st], inc=s_s)
            ev_marks[f] = s_s.n
            b.op("scalar", (lambda ff: lambda s: s.dma_start(
                out=d_s1[ff * 128:(ff + 1) * 128, :], in_=obuf[:, ff % 2, :]))(f),
                waits=[s_s], inc=s_st, dma=True)
    with ExitStack() as _es6:
        _phase6(_es6)
    barrier()

    def _phase7(ph):
        F = ph.enter_context
        xn1b = F(sbt("p7_xn1", [128, NCH, P], BF16))
        hid = F(sbt("p7_hid", [128, NFS, P], BF16))
        rw = F(sbt("p7_rw", [128, 2, NCH, 128], BF16))
        wq = F(sbt("p7_wq", [128, 2, NCH, 128], BF16))
        rwd = F(sbt("p7_rwd", [128, 2, NFS, 128], BF16))
        wqd = F(sbt("p7_wqd", [128, 2, NFS, 128], BF16))
        sgb = F(sbt("p7_sgb", [128, NFS], F32))
        sub = F(sbt("p7_sub", [128, NFS], F32))
        sdb = F(sbt("p7_sdb", [128, NCH], F32))
        sg_t = F(sbt("p7_sgt", [128, 2, 512], F32))
        gt = F(sbt("p7_gt", [128, 2, 512], F32))
        xs_t = F(sbt("p7_xst", [128, 1, P], F32))
        ss_t = F(sbt("p7_sst", [128, 1, P], F32))
        ost = F(sbt("p7_ost", [128, 2, P], F32))
        p_h = F(pst("p7_h", [128, 8, 512], F32))

        b.op("gpsimd", lambda g: g.dma_start(
            out=xn1b[:, :, :], in_=fm(d_xn1)[:, :, W1:EP]), inc=s_gld, dma=True)
        b.op("sync", lambda sy: sy.dma_start(out=sgb[:], in_=sgp), inc=s_yld, dma=True)
        b.op("sync", lambda sy: sy.dma_start(out=sub[:], in_=sup), inc=s_yld, dma=True)
        b.op("sync", lambda sy: sy.dma_start(out=sdb[:], in_=sdp), inc=s_yld, dma=True)
        P7L = (s_gld, s_gld.n)
        P7C = (s_yld, s_yld.n)

        nl = 0
        p7_s = {}
        p7_v = {}
        for f in range(NFS):
            for which, wsrc in ((0, b_wg), (1, b_wu)):
                qn7 = "sync" if which == 0 else "gpsimd"
                qs7 = s_yld if which == 0 else s_gld
                b.op(qn7, (lambda s_, l_: lambda g: g.dma_start(
                    out=rw[:, l_ % 2, :, :],
                    in_=s_.rearrange("p (c f) -> p c f", c=NCH)))(wsrc[f], nl),
                    waits=([(s_v, s_v.n - 1)] if nl >= 2 else []), inc=qs7, dma=True)
                b.op("vector", (lambda l_: lambda v: v.tensor_scalar_add(
                    wq[:, l_ % 2, :, :], rw[:, l_ % 2, :, :], -128.0))(nl),
                    waits=[qs7, (s_t, s_t.n)], inc=s_v)
                deq_mark = s_v.n
                def hmm(l_, wh, fp):
                    def go(t):
                        for st in range(2):
                            for c in range(NCH):
                                ins = t.matmul(p_h[:, 4 * fp + 2 * wh + st, :],
                                               wq[:, l_ % 2, c, :], xn1b[:, c, ST[st]],
                                               start=(c == 0), stop=(c == NCH - 1))
                        return ins
                    return go
                b.op("tensor", hmm(nl, which, f % 2),
                     waits=[(s_v, max(deq_mark, p7_v.get(f - 2, 0))), P7L,
                            (s_s, p7_s.get(f - 2, 0))],
                     inc=s_t)
                nl += 1
            HMM = (s_t, s_t.n)
            for st in range(2):
                b.op("scalar", (lambda ff, ss, fp: lambda s: s.activation(
                    sg_t[:, ss, :], p_h[:, 4 * fp + ss, :], AF.Silu,
                    scale=sgb[:, ff:ff + 1]))(f, st, f % 2),
                    waits=[HMM, P7C], inc=s_s)
                b.op("vector", (lambda ff, ss, fp: lambda v: v.scalar_tensor_tensor(
                    hid[:, ff, ST[ss]], p_h[:, 4 * fp + 2 + ss, :], sub[:, ff:ff + 1],
                    sg_t[:, ss, :], op0=ALU.mult, op1=ALU.mult))(f, st, f % 2),
                    waits=[s_s], inc=s_v)
            p7_s[f] = s_s.n
            p7_v[f] = s_v.n
        HID = (s_v, s_v.n)

        nd = 0
        for c in range(NCH):
            b.op("gpsimd", (lambda s_, l_: lambda g: g.dma_start(
                out=rwd[:, l_ % 2, :, :],
                in_=s_.rearrange("p (f c) -> p f c", f=NFS)))(b_wd[c], nd),
                waits=([(s_v, s_v.n - 3)] if nd >= 2 else []), inc=s_gld, dma=True)
            b.op("vector", (lambda l_: lambda v: v.tensor_scalar_add(
                wqd[:, l_ % 2, :, :], rwd[:, l_ % 2, :, :], -128.0))(nd),
                waits=[s_gld, (s_t, s_t.n)], inc=s_v)
            b.op("sync", (lambda cc: lambda sy: sy.dma_start(
                out=xs_t[:, 0, :], in_=d_x1[cc * 128:(cc + 1) * 128, W1:EP]))(c),
                waits=([(s_v, s_v.n)] if c >= 1 else []), inc=s_yld, dma=True)
            b.op("sync", (lambda cc: lambda sy: sy.dma_start(
                out=ss_t[:, 0, :], in_=d_s1[cc * 128:(cc + 1) * 128, :]))(c),
                inc=s_yld, dma=True)
            def ymm(l_, cc):
                def go(t):
                    for st in range(2):
                        for fr in range(NFS):
                            ins = t.matmul(p_h[:, st, :], wqd[:, l_ % 2, fr, :],
                                           hid[:, fr, ST[st]],
                                           start=(fr == 0), stop=(fr == NFS - 1))
                    return ins
                return go
            b.op("tensor", ymm(nd, c), waits=[s_v, HID, s_s], inc=s_t)
            YMM = (s_t, s_t.n)
            nd += 1
            for st in range(2):
                b.op("scalar", (lambda cc, ss: lambda s: s.activation(
                    gt[:, ss, :], p_h[:, ss, :], AF.Silu, scale=sdb[:, cc:cc + 1]))(c, st),
                    waits=[YMM, P7C], inc=s_s)
                b.op("vector", (lambda cc, ss: lambda v: v.tensor_tensor(
                    ost[:, cc % 2, ST[ss]], ss_t[:, 0, ST[ss]], gt[:, ss, :],
                    ALU.mult))(c, st), waits=[s_s, s_yld, s_st], inc=s_v)
                b.op("vector", (lambda cc, ss: lambda v: v.tensor_tensor(
                    ost[:, cc % 2, ST[ss]], ost[:, cc % 2, ST[ss]], xs_t[:, 0, ST[ss]],
                    ALU.add))(c, st), waits=[s_v], inc=s_v)
            b.op("sync", (lambda cc: lambda sy: sy.dma_start(
                out=o_out[cc * 128:(cc + 1) * 128, :], in_=ost[:, cc % 2, :]))(c),
                waits=[s_v], inc=s_st, dma=True)

    with ExitStack() as _es7:
        _phase7(_es7)
    if DEBUG_FUSED:
        taps = [("dbg_k", d_k, [DKV, T], BF16), ("dbg_va", d_va, [T, NH * 65], BF16),
                ("dbg_q", d_q, [C, EP], BF16),
                ("dbg_xn", d_xn, [C, EP], BF16), ("dbg_ad", d_ad, [C, EP], F32),
                ("dbg_am", d_am, [C, EP], BF16), ("dbg_g", d_g, [C, EP], F32),
                ("dbg_x1", d_x1, [C, EP], F32), ("dbg_xn1", d_xn1, [C, EP], BF16),
                ("dbg_k1", d_k1, [C, EP], BF16), ("dbg_v1", d_v1, [EP, C], BF16),
                ("dbg_s1", d_s1, [C, P], F32)]
        for nm, ap_, shp, dt_ in taps:
            oo = nc.dram_tensor(nm, shp, dt_, kind="ExternalOutput").ap()
            b.op("sync", (lambda o_, a_: lambda sy: sy.dma_start(out=o_, in_=a_))(oo, ap_),
                 waits=[(s_st, s_st.n)], inc=s_st, dma=True)
    b.wait("sync", s_st)
    b.finalize(ctx.enter_context(nc.Block()))
    ctx.close()
    return nc



def tile_cols(WT, colw=128):
    K, Fd = WT.shape
    a = WT.reshape(K // 128, 128, Fd // colw, colw)
    return np.ascontiguousarray(a.transpose(2, 1, 0, 3).reshape(Fd // colw, 128,
                                                                (K // 128) * colw))


def sb_image(WT):
    K, Fd = WT.shape
    a = WT.reshape(K // 128, 128, Fd)
    return np.ascontiguousarray(a.transpose(1, 0, 2).reshape(128, (K // 128) * Fd))


def prep_fused(inp):
    x = inp["L0_attn_x"]
    wqT = tile_cols(inp["L0_attn_Wq"].T)
    wkT = sb_image(inp["L0_attn_Wk"].T)
    wvT = sb_image(inp["L0_attn_Wv"].T)
    memi = sb_image(inp["L0_attn_memory"])
    memn = np.ascontiguousarray(inp["L0_attn_memory_norm"].reshape(4, 128).T)
    mw = np.ascontiguousarray(inp["L0_attn_memory_weight"].reshape(1, 1))
    gwT = np.ascontiguousarray(inp["L0_ffn_gate_w"].T)
    sgwT = np.ascontiguousarray(inp["L0_ffn_shared_gate_w"].reshape(16, 128).T)
    fc1T = np.ascontiguousarray(np.stack(
        [tile_cols(inp["L0_ffn_expert_fc1"][e].T) for e in range(NE)]))
    fc2T = np.ascontiguousarray(np.stack(
        [tile_cols(inp["L0_ffn_expert_fc2"][e].T) for e in range(NE)]))
    pjT = np.ascontiguousarray(np.stack(
        [tile_cols(inp["L0_ffn_expert_proj"][e].T) for e in range(NE)]))
    sf1T = tile_cols(inp["L0_ffn_shared_fc1"].T)
    sf2T = tile_cols(inp["L0_ffn_shared_fc2"].T)
    spT = tile_cols(inp["L0_ffn_shared_proj"].T)
    wq1 = tile_cols(inp["L1_attn_Wq"].T)
    wk1 = tile_cols(inp["L1_attn_Wk"].T)
    wv1 = tile_cols(inp["L1_attn_Wv"].T, colw=512)
    wo1 = tile_cols(inp["L1_attn_Wo"].T)
    wgT = tile_cols(inp["L1_ffn_wq_gate"].T)
    wuT = tile_cols(inp["L1_ffn_wq_up"].T)
    wdT = tile_cols(inp["L1_ffn_wq_down"].T)
    sgp = np.ascontiguousarray(inp["L1_ffn_s_gate"].reshape(NFS, 128).T)
    sup = np.ascontiguousarray(inp["L1_ffn_s_up"].reshape(NFS, 128).T)
    sdp = np.ascontiguousarray(inp["L1_ffn_s_down"].reshape(NCH, 128).T)
    pim = np.zeros((128, 128), np.float32)
    for i in range(64):
        pim[2 * i + 1, 2 * i] = -1.0
        pim[2 * i, 2 * i + 1] = 1.0
    ones128 = np.ones((128, 1), np.float32)
    ones1 = np.ones((1, 128), np.float32)
    onesq = np.ones((128, 128), np.float32)
    eps = np.full((1, 1), 1e-5, np.float32)
    eye = np.eye(128, dtype=np.float32)
    sel = np.zeros((9, 9 * 128), np.float32)
    for e in range(9):
        sel[e, e * 128:(e + 1) * 128] = 1.0
    tril = np.tril(np.ones((W1, W1), np.float32))
    base = np.concatenate([np.ones((W1, W1), np.float32), tril], axis=1)
    first = np.concatenate([np.zeros((W1, W1), np.float32), tril], axis=1)

    def mk_mask(m):
        mk = m.T.reshape(4, 128, 256)
        md = np.stack([mk, mk], axis=2)
        md = md.transpose(1, 0, 2, 3)
        return np.ascontiguousarray(md.reshape(128, 4 * 2 * 256).astype(bfdt))

    mbase = mk_mask(base)
    mfirst = mk_mask(first)
    inv = 1.0 / 10000.0 ** (np.arange(0, DH1, 2, dtype=np.float32) / DH1)
    xTb = [np.ascontiguousarray(x[bb].T) for bb in range(B)]
    maps = []
    for core, bb, t0 in _cores():
        if core % 4 == 0:
            pre = x[bb, t0:t0 + W1, :].T
            ppos = np.arange(t0 - W1, t0, dtype=np.float32)
        else:
            pre = x[bb, t0 - W1:t0, :].T
            ppos = np.arange(t0 - W1, t0, dtype=np.float32)
        xTe = np.ascontiguousarray(np.concatenate([pre, x[bb, t0:t0 + P, :].T], axis=1))
        tpos = np.concatenate([ppos, np.arange(t0, t0 + P, dtype=np.float32)])
        fr = np.outer(tpos, inv).astype(np.float32)
        ctab = np.empty((128, EP), np.float32)
        stabv = np.empty((128, EP), np.float32)
        ctab[0::2, :] = np.cos(fr).T
        ctab[1::2, :] = np.cos(fr).T
        stabv[0::2, :] = np.sin(fr).T
        stabv[1::2, :] = np.sin(fr).T
        maps.append(dict(
            xTf=xTb[bb], xTe=xTe, wqT=wqT, wkT=wkT, wvT=wvT, memi=memi, memn=memn,
            mw=mw, gwT=gwT, sgwT=sgwT, fc1T=fc1T, fc2T=fc2T, pjT=pjT, sf1T=sf1T,
            sf2T=sf2T, spT=spT, wq1=wq1, wk1=wk1, wv1=wv1, wo1=wo1, wgT=wgT,
            wuT=wuT, wdT=wdT, sgp=sgp, sup=sup, sdp=sdp, ctab=ctab, stab=stabv,
            pim=pim, m0=(mfirst if core % 4 == 0 else mbase), mb=mbase,
            c_ones128=ones128, c_ones1=ones1, c_eps=eps, c_eye=eye,
            c_onesq=onesq, c_sel=sel))
    return maps


def _cores():
    for core in range(NCORES):
        yield core, core // 4, (core % 4) * P


def prep_A(inp):
    x = inp["L0_attn_x"]
    wqT = np.ascontiguousarray(inp["L0_attn_Wq"].T)
    wkT = np.ascontiguousarray(inp["L0_attn_Wk"].T)
    wvT = np.ascontiguousarray(inp["L0_attn_Wv"].T)
    memi = np.ascontiguousarray(inp["L0_attn_memory"])
    memn = np.ascontiguousarray(inp["L0_attn_memory_norm"].reshape(4, 128).T)
    ones128 = np.ones((128, 1), np.float32)
    ones1 = np.ones((1, 128), np.float32)
    eps = np.full((1, 1), 1e-5, np.float32)
    maps = []
    for core, bb, t0 in _cores():
        xTc = np.ascontiguousarray(x[bb, t0:t0 + P, :].T)
        maps.append(dict(xT=xTc, wqT=wqT, wkT=wkT, wvT=wvT, memi=memi, memn=memn,
                         c_ones128=ones128, c_ones1=ones1, c_eps=eps))
    return maps


def kernel(**inputs):
    cores = list(range(NCORES))
    res = run_bass_kernel_spmd(build_fused(False), prep_fused(inputs),
                               core_ids=cores).results
    out = np.empty((B, T, C), np.float32)
    for core, bb, t0 in _cores():
        out[bb, t0:t0 + P, :] = np.asarray(res[core]["outT"], np.float32).T
    return out
